# version 12
# baseline (speedup 1.0000x reference)
from contextlib import ExitStack

import numpy as np
import ml_dtypes
import concourse.bass as bass
import concourse.mybir as mybir
from concourse.bass_utils import run_bass_kernel_spmd

F32 = mybir.dt.float32
BF16 = mybir.dt.bfloat16
AF = mybir.ActivationFunctionType
ALU = mybir.AluOpType
AX = mybir.AxisListType
BF = ml_dtypes.bfloat16

HID = 2048
NH = 32
DK = 1024
DV = 2048
RLO = 32
GLO = 64
EPS = 1e-5
NCC = HID // 128
CH = 32
N_CORES = 8
F_MOE = 1408
F_SH = 5632
NFC = F_MOE // 128
NE = 8
QW = 256
LB = 128

_DRAIN_OPS = {
    "tensor_copy", "tensor_tensor", "tensor_scalar", "tensor_single_scalar",
    "tensor_scalar_add", "tensor_scalar_sub", "tensor_scalar_mul",
    "tensor_scalar_max", "tensor_scalar_min", "tensor_add", "tensor_sub",
    "tensor_mul", "tensor_max", "tensor_reduce", "reduce_max", "reduce_sum",
    "scalar_tensor_tensor", "tensor_tensor_scan", "select", "reciprocal",
    "memset", "memzero", "activation", "copy", "mul", "add", "square", "sqrt",
    "sign", "bn_stats", "bn_aggr",
}


class Eng:

    def __init__(self, eng):
        self._e = eng
        self._dirty = False

    def __getattr__(self, name):
        attr = getattr(self._e, name)
        if name in _DRAIN_OPS:
            def wrapped(*a, **k):
                if self._dirty:
                    self._e.drain()
                self._dirty = True
                return attr(*a, **k)
            return wrapped
        return attr


class TEng:

    def __init__(self, eng):
        self._e = eng
        self._prev_key = None

    @staticmethod
    def _placement(out, lhsT, rhs, tp):
        bases = []
        for ap in (out, lhsT, rhs):
            try:
                bases.append(ap.base_partition())
            except Exception:
                bases.append(-1)
        if tp is None and bases == [0, 0, 0]:
            return None
        return (tp, tuple(bases))

    def matmul(self, out, lhsT, rhs, **kw):
        key = self._placement(out, lhsT, rhs, kw.get("tile_position"))
        if key != self._prev_key and (key is not None
                                      or self._prev_key is not None):
            self._e.drain()
        self._prev_key = key
        return self._e.matmul(out, lhsT, rhs, **kw)

    def transpose(self, *a, **kw):
        if self._prev_key is not None:
            self._e.drain()
            self._prev_key = None
        return self._e.transpose(*a, **kw)

    def __getattr__(self, name):
        return getattr(self._e, name)


class KB:

    def __init__(self, T, n_cores, debug=False):
        assert T % 128 == 0
        self.T = T
        self.NT = T // 128
        self.NCH = T // CH
        self.NG = T // 128
        self.n_cores = n_cores
        self.debug = debug
        self.nc = bass.Bass("TRN2", target_bir_lowering=False)
        self.stack = ExitStack()
        self.D = {}
        self.dbg_outs = []
        self.sems = {}
        self.c = {}
        self._declare_dram()
        self._alloc_sbuf()
        self._alloc_sems()

    def inc(self, instr, name, n=1):
        instr.then_inc(self.sems[name], n)
        self.c[name] += n
        return self.c[name]

    def w(self, eng, name, val):
        eng.wait_ge(self.sems[name], val)

    def din(self, name, shape, dt=F32):
        self.D[name] = self.nc.dram_tensor(
            name, list(shape), dt, kind="ExternalInput").ap()

    def dint(self, name, shape, dt=F32):
        self.D[name] = self.nc.dram_tensor(name, list(shape), dt).ap()

    def sb(self, name, shape, dt):
        return self.stack.enter_context(
            self.nc.sbuf_tensor(name, list(shape), dt)).ap()

    def ps(self, stk, name, shape, dt=F32):
        self._uid = getattr(self, "_uid", 0) + 1
        return stk.enter_context(
            self.nc.psum_tensor(f"{name}_{self._uid}", list(shape), dt)).ap()

    def dbg(self, name, ap):
        if not self.debug:
            return
        d = self.nc.dram_tensor("dbg_" + name, list(ap.shape), ap.dtype,
                                kind="ExternalOutput").ap()
        self.dbg_outs.append("dbg_" + name)
        k = self
        with self.nc.Block() as blk:
            @blk.sync
            def _(s):
                n = k.inc(s.dma_start(out=d, in_=ap), "o_dma", 16)
                k.w(s, "o_dma", n)

    def mm(self, t, out, lhsT, rhs, start=True, stop=True, tp=None):
        N = rhs.shape[-1]
        last = None
        o = 0
        while o < N:
            w_ = min(512, N - o)
            last = t.matmul(out[:, o:o + w_], lhsT, rhs[:, o:o + w_],
                            start=start, stop=stop, tile_position=tp)
            o += w_
        return last

    def _declare_dram(self):
        T = self.T
        din, dint = self.din, self.dint
        din("xsl", [1 + LB + T, HID])
        din("wx1", [128, NCC * 160], BF16)
        din("wx2a", [128, HID], BF16)
        din("wx2b", [32, HID], BF16)
        din("ww1", [128, NCC * GLO], BF16)
        din("ww2", [64, DK], BF16)
        din("wrq", [DK // QW, 128, NCC * QW])
        din("wkq", [DK // QW, 128, NCC * QW])
        din("wvq", [DV // QW, 128, NCC * QW])
        din("wgs", [16, 128, NCC * 128])
        din("wos", [16, 128, HID])
        din("bw", [128, 8], F32)
        din("bonus", [128, 8], F32)
        din("muxc", [128, NCC], F32)
        din("gnw_row", [1, DV], F32)
        din("gnb_row", [1, DV], F32)
        din("ident", [128, 128], BF16)
        din("maskp", [128, 128], BF16)
        din("blkh4", [128, 4], BF16)
        din("onesb", [128, 1], BF16)
        din("gws", [128, NCC * NE])
        din("sgws", [128, NCC])
        din("fc1s", [NE, NFC, 128, NCC * 128])
        din("fc2s", [NE, NFC, 128, NCC * 128])
        din("pjs", [NE, NFC, 128, HID])
        din("sf1s", [4 * NFC, 128, NCC * 128])
        din("sf2s", [4 * NFC, 128, NCC * 128])
        din("spjs", [4 * NFC, 128, HID])
        self.out = self.nc.dram_tensor("out", [T, HID], F32,
                                       kind="ExternalOutput").ap()
        dint("x1d", [T, HID])

    def _alloc_sbuf(self):
        T, NT, NCH, NG = self.T, self.NT, self.NCH, self.NG
        sb = self.sb
        AW = max(16 * T, 16384)
        self.AW = AW
        self.arA = sb("arA", [128, AW], BF16)
        self.arB = sb("arB", [128, AW], BF16)
        self.arC = sb("arC", [128, AW], BF16)
        self.XNT_t = sb("xnt", [128, max(NCC * (T + 1), 16384)], BF16)
        self.V = sb("vtm", [128, AW], BF16)
        self.WB = sb("wb0", [128, 2 * NCC * QW], BF16)
        arAf = self.arA.bitcast(F32)
        self.arAf = arAf
        self.arBf = self.arB.bitcast(F32)
        self.arCf = self.arC.bitcast(F32)
        WBf = self.WB.bitcast(F32)

        self._bind_views(T)

        self.Z0 = self.V[:, 0:T]
        self.Z1 = sb("z1", [32, T], BF16)
        self.WX1 = self.V[:, 4096:4096 + NCC * 160]
        self.WX2A = sb("wx2as", [128, HID], BF16)
        self.WX2B = sb("wx2bs", [32, HID], BF16)
        self.WW1 = sb("ww1s", [128, NCC * GLO], BF16)
        self.WW2 = sb("ww2s", [64, DK], BF16)
        self.MUXC = sb("muxcs", [128, NCC], F32)
        self.BW = sb("bws", [128, 8], F32)
        self.BONUS = sb("bonuss", [128, 8], F32)
        self.IDENT = sb("idents", [128, 128], BF16)
        self.MASKP = sb("maskps", [128, 128], BF16)
        self.BLKH4 = sb("blkh4s", [128, 4], BF16)
        self.ONESB = sb("onesbs", [128, 1], BF16)
        self.S = sb("ssb", [128, 512], F32)
        self.SB16T = sb("sb16", [128, 1024], BF16)
        self.SB16B = [self.SB16T[:, 0:512], self.SB16T[:, 512:1024]]
        self.MMB = sb("mmbs", [128, 8 * (NCH + 1)], F32)
        self.DEC = sb("decs", [128, 8 * NCH], F32)
        self.DTS = self.V[0:4, 1024:1024 + T]
        self.DTM = sb("dtms", [128, NG * 32], BF16)
        self.HALOT = sb("halots", [128, NCC], BF16)
        self.ST1 = sb("st1", [128, 16], F32)
        self.GSIL = sb("gsils", [128, max(T, 1024)], BF16)
        self.SILS = sb("sils", [128, max(T, 1024)], BF16)
        self.DELTA = sb("deltas", [128, 1024], BF16)
        self.DELTAG = sb("deltag", [128, 2048], BF16)
        self.KHTG = self.DELTA[:, 0:1024]
        self.KHTGB = [self.DELTA[:, 0:1024], self.SILS[:, 0:1024]]
        self.GWSB = sb("gwsb", [128, NCC * NE], BF16)
        self.SGWB = sb("sgwb", [128, NCC], BF16)
        self.THB = self.GSIL
        self.RKB = self.SILS
        gsf = self.GSIL.bitcast(F32)
        self.GNS = gsf[:, 0:128]
        self.CMB = gsf[:, 128:128 + 8 * NT]
        self.SGC = gsf[:, 224:224 + NT]
        self.PR8 = gsf[:, 240:252]
        self.STW = gsf[:, 252:260]

    def _bind_views(self, T):
        self.T = T
        self.NT = max(1, T // 128)
        self.NCH = T // CH
        self.NG = max(1, T // 128)
        NT, NCH, NG = self.NT, self.NCH, self.NG
        arAf = self.arAf
        WBf = self.WB.bitcast(F32)
        self.XNT = self.XNT_t[:, 0:NCC * (T + 1)]

        self.XT = [arAf[:, 0:HID], arAf[:, HID:2 * HID]]
        self.SCF = arAf[:, 2 * HID:3 * HID]
        self.X1T = self.V.bitcast(F32)[:, HID:2 * HID]
        self.SCR = [self.arA[:, 6 * HID:7 * HID], self.arA[:, 7 * HID:8 * HID]]
        self.MUX = self.arA[:, 0:NCC * 512]
        self.WEXPC = arAf[:, 4096:4096 + T]
        self.CGL = arAf[:, 4096 + T:4096 + 2 * T]
        self.CNL = arAf[:, 4096 + 2 * T:4096 + 3 * T]
        self.E1S = self.arA[:, 8192 + 6 * T:8192 + 7 * T]
        self.E2S = self.arA[:, 8192 + 7 * T:8192 + 8 * T]
        self.MUXF = self.arB
        self.Rb = self.arB[:, 0:8 * T]
        self.Kb = self.arB[:, 8 * T:16 * T]
        self.QT = self.arC[:, 0:8 * T]
        self.KH = self.arC[:, 8 * T:16 * T]
        self.PM = self.arB[:, 0:8 * T]
        self.Og = [self.arA[:, gr * DV:(gr + 1) * DV] for gr in range(NG)]
        self.ONTM = self.arA[:, 0:16 * T]
        self.ONFM = self.arB[:, 0:16 * T]
        self.XIN = self.V[:, 0:16 * T]
        self.XFF = self.arC[:, 0:16 * T]
        self.Ht = self.V[:, 0:NFC * T]
        self.SQB = self.V[:, 0:DV]
        self.SQC = self.arB[:, 8 * T:8 * T + DV]
        self.Yt = [self.arA[:, tt * HID:(tt + 1) * HID] for tt in range(NT)]

        self.GNW = WBf[:, 0:DV]
        self.GNB = WBf[:, DV:2 * DV]
        self.WOSL = [self.WB[:, 0:2048], self.WB[:, 2048:4096]]
        self.WGSL = [self.WB[:, 4096:6144], self.WB[:, 6144:8192]]
        self.FC1SL = [self.WB[:, 0:2048], self.WB[:, 2048:4096],
                      self.arB[:, 8192:10240], self.arB[:, 10240:12288]]
        self.FC2SL = [self.WB[:, 4096:6144], self.WB[:, 6144:8192],
                      self.arB[:, 12288:14336], self.arB[:, 14336:16384]]


    def _alloc_sems(self):
        names = [
            "cst", "cstg", "wq0", "wq1", "wq2", "wq3", "wfree", "xld0", "xld1", "v_a", "s_a", "tr_mm", "tr_ev",
            "mu_mm", "xin_v", "xin_g", "fe_mm", "fe_ev", "dec_v", "dec_s",
            "e_a",
            "sum_mm", "sum_up", "ccs", "fld_d", "p_mm", "pm_ev",
            "c_mm", "c_up", "c_ev", "c_oi", "gn_v", "lg_mm", "lg_s", "lg_v",
            "wo_mm", "wo_v", "r_mm", "h_mm", "h_s", "h_v",
            "y_mm", "y_v", "o_dma", "sp_d", "gq0", "gq1", "woq0", "woq1", "wor",
            "fq0", "fq1", "fq2", "fq3", "f2q0", "f2q1", "f2q2", "f2q3",
            "pq0", "pq1", "pq2", "pq3", "x1s", "fld0", "fld1",
            "rt_mm", "rt_v",
        ]
        for n in names:
            self.sems[n] = self.stack.enter_context(self.nc.semaphore(n))
            self.c[n] = 0

    def _q_slot(self, gi):
        W = NCC * QW
        return self.WB[:, (gi % 2) * W:(gi % 2 + 1) * W]

    def _q_load(self, g, gi):
        name, idx = self._q_list[gi]
        if gi >= 2:
            sem, val = self.qfree[gi - 2]
            self.w(g, sem, val)
        self.inc(g.dma_start(out=self._q_slot(gi), in_=self.D[name][idx]),
                 f"wq{gi % 2}", 16)

    def _q_wait(self, eng, gi):
        self.w(eng, f"wq{gi % 2}", 16 * (gi // 2 + 1))

    def _wo_q(self, dc):
        if dc < 8:
            return self.XNT_t[:, dc * 2048:(dc + 1) * 2048]
        if dc < 10:
            return self.WB[:, (dc - 8) * 2048:(dc - 7) * 2048]
        if dc < 12:
            return self.V[:, (dc - 10) * 2048:(dc - 9) * 2048]
        if dc < 14:
            return self.V[:, 12288 + (dc - 12) * 2048:
                          12288 + (dc - 11) * 2048]
        return self.WB[:, 4096 + (dc - 14) * 2048:4096 + (dc - 13) * 2048]

    def ph_load_consts(self):
        k = self
        with self.nc.Block() as blk:
            @blk.sync
            def _(s):
                for n, dst in [
                        ("muxc", k.MUXC), ("bw", k.BW), ("bonus", k.BONUS),
                        ("ident", k.IDENT), ("maskp", k.MASKP),
                        ("blkh4", k.BLKH4), ("onesb", k.ONESB),
                        ("wx1", k.WX1),
                        ("wx2a", k.WX2A), ("wx2b", k.WX2B), ("ww1", k.WW1),
                        ("ww2", k.WW2)]:
                    k.inc(s.dma_start(out=dst, in_=k.D[n]), "cst", 16)
                k.w(s, "cst", k.c["cst"])

            @blk.gpsimd
            def _(g):
                k.inc(g.dma_start(out=k.GWSB, in_=k.D["gws"]), "cstg", 16)
                k.inc(g.dma_start(out=k.SGWB, in_=k.D["sgws"]), "cstg", 16)
                for gi in range(2):
                    self._q_load(g, gi)
                k.w(g, "cstg", k.c["cstg"])

    def ph_ln0(self, off=LB):
        k = self
        T, NT = self.T, self.NT
        with self.nc.Block() as blk, ExitStack() as pstk:
            tps = [k.ps(pstk, "tps0", [128, 128], BF16),
                   k.ps(pstk, "tps1", [128, 128], BF16)]
            hps = k.ps(pstk, "hps", [128, NCC], F32)
            s0, v0 = k.c["s_a"], k.c["v_a"]
            t0_, e0 = k.c["tr_mm"], k.c["tr_ev"]
            x0 = [k.c["xld0"], k.c["xld1"]]

            @blk.sync
            def _(s):
                for i in range(NT + 1):
                    if i >= 2:
                        k.w(s, "v_a", v0 + 2 * (i - 1))
                        k.w(s, "s_a", s0 + 2 * i - 3)
                    src = (k.D["xsl"][off + 1 + i * 128:off + 1 + (i + 1) * 128, :]
                           if i < NT else k.D["xsl"][off:off + 1, :])
                    dst = k.XT[i % 2] if i < NT else k.XT[i % 2][0:1, :]
                    k.inc(s.dma_start(out=dst, in_=src), f"xld{i % 2}", 16)

            @blk.vector
            def _(v):
                v = Eng(v)
                for i in range(NT + 1):
                    rows = 128 if i < NT else 1
                    xt = k.XT[i % 2][0:rows, :]
                    k.w(v, f"xld{i % 2}", x0[i % 2] + 16 * (i // 2 + 1))
                    v.tensor_reduce(k.ST1[0:rows, 0:1], xt, AX.X, ALU.add)
                    v.tensor_scalar(k.ST1[0:rows, 1:2], k.ST1[0:rows, 0:1],
                                    1.0 / HID, None, ALU.mult)
                    k.w(v, "s_a", s0 + 2 * i + 1)
                    v.tensor_scalar(k.ST1[0:rows, 2:3], k.ST1[0:rows, 4:5],
                                    1.0 / HID, EPS, ALU.mult, ALU.add)
                    v.tensor_tensor(k.ST1[0:rows, 3:4], k.ST1[0:rows, 1:2],
                                    k.ST1[0:rows, 1:2], ALU.mult)
                    v.tensor_tensor(k.ST1[0:rows, 2:3], k.ST1[0:rows, 2:3],
                                    k.ST1[0:rows, 3:4], ALU.subtract)
                    i1 = v.reciprocal(k.ST1[0:rows, 6:7], k.ST1[0:rows, 2:3])
                    k.inc(i1, "v_a", 1)
                    k.w(v, "s_a", s0 + 2 * i + 2)
                    if i >= 2:
                        k.w(v, "tr_mm", t0_ + (i - 1) * NCC)
                    i2 = v.tensor_scalar(k.SCR[i % 2][0:rows, :], xt,
                                         k.ST1[0:rows, 1:2], k.ST1[0:rows, 5:6],
                                         ALU.subtract, ALU.mult)
                    k.inc(i2, "v_a", 1)
                    if 1 <= i <= NT:
                        for cc in range(NCC):
                            n = (i - 1) * NCC + cc
                            k.w(v, "tr_mm", t0_ + n + 1)
                            b = cc * (T + 1)
                            ev = v.tensor_copy(
                                k.XNT[:, b + 1 + (i - 1) * 128:b + 1 + i * 128],
                                tps[n % 2])
                            k.inc(ev, "tr_ev", 1)
                k.w(v, "tr_mm", t0_ + NT * NCC + 1)
                v.tensor_copy(k.HALOT, hps)
                ev = v.tensor_copy(
                    k.XNT.rearrange("p (c t) -> p c t", c=NCC)[:, :, 0:1],
                    k.HALOT.unsqueeze(2))
                k.inc(ev, "tr_ev", 1)

            @blk.scalar
            def _(sc):
                sc = Eng(sc)
                for i in range(NT + 1):
                    rows = 128 if i < NT else 1
                    xt = k.XT[i % 2][0:rows, :]
                    k.w(sc, f"xld{i % 2}", x0[i % 2] + 16 * (i // 2 + 1))
                    a = sc.activation(k.SCF[0:rows, :], xt, AF.Square,
                                      accum_out=k.ST1[0:rows, 4:5])
                    k.inc(a, "s_a", 1)
                    k.w(sc, "v_a", v0 + 2 * i + 1)
                    a = sc.activation(k.ST1[0:rows, 5:6], k.ST1[0:rows, 6:7],
                                      AF.Sqrt)
                    k.inc(a, "s_a", 1)

            @blk.tensor
            def _(t):
                t = TEng(t)
                for i in range(NT):
                    k.w(t, "v_a", v0 + 2 * i + 2)
                    for cc in range(NCC):
                        n = i * NCC + cc
                        if n >= 2:
                            k.w(t, "tr_ev", e0 + n - 1)
                        mm = t.transpose(tps[n % 2],
                                         k.SCR[i % 2][:, cc * 128:(cc + 1) * 128],
                                         k.IDENT)
                        k.inc(mm, "tr_mm", 1)
                k.w(t, "v_a", v0 + 2 * NT + 2)
                for cc in range(NCC):
                    mm = t.matmul(hps[:, cc:cc + 1],
                                  k.SCR[NT % 2][0:1, cc * 128:(cc + 1) * 128],
                                  k.ONESB[0:1, 0:1], start=True, stop=True)
                k.inc(mm, "tr_mm", 1)

    def ph_z(self):
        k = self
        T = self.T
        with self.nc.Block() as blk, ExitStack() as pstk:
            zp0 = k.ps(pstk, "zp0", [128, T], F32)
            zp1 = k.ps(pstk, "zp1", [32, T], F32)
            xin0, mu0 = k.c["xin_v"], k.c["mu_mm"]

            @blk.vector
            def _(v):
                v = Eng(v)
                for cc in range(NCC):
                    b = cc * (T + 1)
                    if cc >= 1:
                        k.w(v, "mu_mm", mu0 + cc)
                    dl = k.DELTA[:, 0:T]
                    v.tensor_tensor(dl, k.XNT[:, b:b + T],
                                    k.XNT[:, b + 1:b + T + 1], ALU.subtract)
                    i2 = v.scalar_tensor_tensor(dl, dl,
                                                k.MUXC[:, cc:cc + 1],
                                                k.XNT[:, b + 1:b + T + 1],
                                                ALU.mult, ALU.add)
                    k.inc(i2, "xin_v", 1)

            @blk.tensor
            def _(t):
                t = TEng(t)
                for cc in range(NCC):
                    k.w(t, "xin_v", xin0 + cc + 1)
                    st, sp = cc == 0, cc == NCC - 1
                    k.mm(t, zp0, k.WX1[:, cc * 160:cc * 160 + 128],
                         k.DELTA[:, 0:T], start=st, stop=sp)
                    last = k.mm(t, zp1[0:32, :],
                                k.WX1[:, cc * 160 + 128:cc * 160 + 160],
                                k.DELTA[:, 0:T], start=st, stop=sp)
                    k.inc(last, "mu_mm", 1)

            @blk.scalar
            def _(sc):
                sc = Eng(sc)
                k.w(sc, "mu_mm", k.c["mu_mm"])
                sc.activation(k.Z0[:, 0:T], zp0, AF.Tanh)
                sc.activation(k.Z1[0:32, 0:T], zp1[0:32, :], AF.Tanh)

    def _xin_sub(self, ge, cc, tw, xv0):
        k = self
        b = cc * (self.T + 1)
        dl = k.DELTAG[:, (cc % 2) * 1024:(cc % 2) * 1024 + tw]
        if cc >= 2:
            k.w(ge, "xin_v", xv0 + cc - 1)
        i = ge.tensor_tensor(dl, k.XNT[:, b:b + tw],
                             k.XNT[:, b + 1:b + tw + 1], ALU.subtract)
        k.inc(i, "xin_g", 1)

    def _xin_mu(self, v, cc, tw, mup, dst, dst_col):
        k = self
        b = cc * (self.T + 1)
        dl = k.DELTAG[:, (cc % 2) * 1024:(cc % 2) * 1024 + tw]
        v.tensor_tensor(dl, dl, mup, ALU.mult)
        return v.tensor_tensor(dst[:, dst_col:dst_col + tw], dl,
                               k.XNT[:, b + 1:b + tw + 1], ALU.add)

    def _xin_dve(self, v, cc, t0, tw, mup, dst, dst_col, tmp=None):
        k = self
        dl = (tmp if tmp is not None else k.DELTA)[:, 0:tw]
        b = cc * (self.T + 1) + t0
        v.tensor_tensor(dl, k.XNT[:, b:b + tw],
                        k.XNT[:, b + 1:b + tw + 1], ALU.subtract)
        v.tensor_tensor(dl, dl, mup, ALU.mult)
        return v.tensor_tensor(dst[:, dst_col:dst_col + tw], dl,
                               k.XNT[:, b + 1:b + tw + 1], ALU.add)

    def ph_branch_fm(self, br, qbase, dst, out_w, prefetch):
        k = self
        T, NT = self.T, self.NT
        n_os = max(1, out_w // 128)
        qd = max(1, QW // 128)
        rows = 128 if out_w >= 128 else out_w
        MUXT = self.arA[:, 0:NCC * T]
        with self.nc.Block() as blk, ExitStack() as pstk:
            mup = [k.ps(pstk, "mup0", [128, T], F32),
                   k.ps(pstk, "mup1", [128, T], F32)]
            pc = [k.ps(pstk, "pc0", [rows, T], F32),
                  k.ps(pstk, "pc1", [rows, T], F32)]
            mu0, xin0 = k.c["mu_mm"], k.c["xin_v"]
            xg0 = k.c["xin_g"]
            fe0, ev0 = k.c["fe_mm"], k.c["fe_ev"]
            tp = (32 * br, 0)

            @blk.tensor
            def _(t):
                t = TEng(t)
                for cc in range(NCC):
                    if cc >= 2:
                        k.w(t, "xin_v", xin0 + cc - 1)
                    mm = k.mm(t, mup[cc % 2],
                              k.WX2A[32 * br:32 * br + 32,
                                     cc * 128:(cc + 1) * 128],
                              k.Z0[32 * br:32 * br + 32, 0:T],
                              start=True, stop=True, tp=tp)
                    k.inc(mm, "mu_mm", 1)
                k.w(t, "xin_v", xin0 + NCC)
                for os in range(n_os):
                    if os >= 2:
                        k.w(t, "fe_ev", ev0 + os - 1)
                    if qbase is not None:
                        self._q_wait(t, qbase + os // qd)
                    for cc in range(NCC):
                        if qbase is not None:
                            slot = self._q_slot(qbase + os // qd)
                            qo = (os % qd) * 128
                            lhsT = slot[:, cc * QW + qo:cc * QW + qo + 128]
                        else:
                            lhsT = k.WW1[:, cc * GLO:(cc + 1) * GLO]
                        mm = k.mm(t, pc[os % 2][0:rows, :], lhsT,
                                  MUXT[:, cc * T:(cc + 1) * T],
                                  start=(cc == 0), stop=(cc == NCC - 1))
                    k.inc(mm, "fe_mm", 1)
                    if qbase is not None and (os % qd) == qd - 1:
                        k.qfree[qbase + os // qd] = ("fe_mm", k.c["fe_mm"])

            @blk.vector
            def _(v):
                v = Eng(v)
                for cc in range(NCC):
                    k.w(v, "mu_mm", mu0 + cc + 1)
                    k.w(v, "xin_g", xg0 + cc + 1)
                    i2 = self._xin_mu(v, cc, T, mup[cc % 2], MUXT, cc * T)
                    k.inc(i2, "xin_v", 1)
                for os in range(n_os):
                    k.w(v, "fe_mm", fe0 + os + 1)
                    ev = v.tensor_copy(dst[0:rows, os * T:(os + 1) * T],
                                       pc[os % 2][0:rows, :])
                    k.inc(ev, "fe_ev", 1)

            @blk.gpsimd
            def _(g):
                ge = Eng(g)
                for cc in range(NCC):
                    self._xin_sub(ge, cc, T, xin0)
                for gi in prefetch:
                    self._q_load(g, gi)

    def ph_w_decay(self, extras=True):
        k = self
        T, NT, NCH, NG = self.T, self.NT, self.NCH, self.NG
        dvk = 3 if extras else 2
        smk = 2 if extras else 1
        ssk = 3
        with self.nc.Block() as blk, ExitStack() as pstk:
            pb = [k.ps(pstk, "pbw0", [128, T], F32),
                  k.ps(pstk, "pbw1", [128, T], F32)]
            if extras:
                dps = k.ps(pstk, "dps", [4, T], F32)
                tp4 = [k.ps(pstk, "tp40", [128, 4], BF16),
                       k.ps(pstk, "tp41", [128, 4], BF16)]
            s0, v0, e0 = k.c["dec_s"], k.c["dec_v"], k.c["e_a"]
            m0 = k.c["sum_mm"]
            t0_, te0 = k.c["tr_mm"], k.c["tr_ev"]
            NB = NCH + 1

            @blk.tensor
            def _(t):
                t = TEng(t)
                for ks in range(8):
                    k.w(t, "dec_s", s0 + 1 + max(0, ssk * (ks - 2) + 1) if ks >= 2
                        else s0 + 1)
                    mm = k.mm(t, pb[ks % 2], k.WW2[0:64, ks * 128:(ks + 1) * 128],
                              k.THB[0:64, 0:T], start=True, stop=True)
                    k.inc(mm, "sum_mm", 1)
                    if not extras:
                        continue
                    k.w(t, "dec_v", v0 + dvk * ks + 3)
                    if ks >= 1:
                        k.w(t, "e_a", e0 + ks)
                    mm = k.mm(t, dps[0:4, :], k.BLKH4, k.RKB[:, 0:T],
                              start=True, stop=True)
                    k.inc(mm, "sum_mm", 1)
                    k.w(t, "e_a", e0 + ks + 1)
                    for gr in range(NG):
                        n = NG * ks + gr
                        if n >= 2:
                            k.w(t, "tr_ev", te0 + n - 1)
                        mm = t.transpose(tp4[n % 2],
                                         k.DTS[0:4, gr * 128:(gr + 1) * 128],
                                         k.IDENT[0:4, 0:4])
                        k.inc(mm, "tr_mm", 1)

            @blk.scalar
            def _(sc):
                sc = Eng(sc)
                a = sc.activation(k.THB[0:64, 0:T], k.THB[0:64, 0:T], AF.Tanh)
                k.inc(a, "dec_s", 1)
                for ks in range(8):
                    k.w(sc, "sum_mm", m0 + smk * ks + 1)
                    if ks >= 1:
                        k.w(sc, "dec_v", v0 + dvk * (ks - 1) + 1)
                    a = sc.activation(k.WEXPC, pb[ks % 2], AF.Exp,
                                      bias=k.BW[:, ks:ks + 1])
                    k.inc(a, "dec_s", 1)
                    k.w(sc, "dec_v", v0 + dvk * ks + 1)
                    sc.activation(k.E2S, k.CNL, AF.Exp)
                    a = sc.activation(k.E1S, k.CGL, AF.Exp, scale=-1.0)
                    k.inc(a, "dec_s", 1)
                    a = sc.activation(k.DEC[:, ks * NCH:(ks + 1) * NCH],
                                      k.DEC[:, ks * NCH:(ks + 1) * NCH],
                                      AF.Exp, scale=-1.0)
                    k.inc(a, "dec_s", 1)

            @blk.vector
            def _(v):
                v = Eng(v)
                for ks in range(8):
                    NBo = ks * NB
                    k.w(v, "dec_s", s0 + 1 + ssk * ks + 1)
                    v.tensor_tensor_scan(k.CGL, k.WEXPC, k.WEXPC, 0.0,
                                         ALU.add, ALU.bypass)
                    v.memset(k.MMB[:, NBo:NBo + 1], 0.0)
                    v.tensor_copy(k.MMB[:, NBo + 1:NBo + NB],
                                  k.CGL[:, CH - 1::CH])
                    v.tensor_tensor(k.DEC[:, ks * NCH:(ks + 1) * NCH],
                                    k.MMB[:, NBo + 1:NBo + NB],
                                    k.MMB[:, NBo:NBo + NB - 1], ALU.subtract)
                    v.tensor_tensor(
                        k.CNL.rearrange("p (a b) -> p a b", a=NCH),
                        k.CGL.rearrange("p (a b) -> p a b", a=NCH),
                        k.MMB[:, NBo:NBo + NCH].unsqueeze(2).broadcast_to(
                            (128, NCH, CH)), ALU.subtract)
                    i1 = v.tensor_tensor(k.CGL, k.CNL, k.WEXPC, ALU.subtract)
                    k.inc(i1, "dec_v", 1)
                    k.w(v, "dec_s", s0 + 1 + ssk * ks + 2)
                    if extras:
                        v.tensor_tensor(k.QT[:, ks * T:(ks + 1) * T],
                                        k.Rb[:, ks * T:(ks + 1) * T], k.E1S,
                                        ALU.mult)
                    i2 = v.tensor_tensor(k.KH[:, ks * T:(ks + 1) * T],
                                         k.Kb[:, ks * T:(ks + 1) * T], k.E2S,
                                         ALU.mult)
                    k.inc(i2, "dec_v", 1)
                    if not extras:
                        continue
                    i3 = v.scalar_tensor_tensor(k.RKB[:, 0:T],
                                                k.Rb[:, ks * T:(ks + 1) * T],
                                                k.BONUS[:, ks:ks + 1],
                                                k.Kb[:, ks * T:(ks + 1) * T],
                                                ALU.mult, ALU.mult)
                    k.inc(i3, "dec_v", 1)
                    k.w(v, "sum_mm", m0 + smk * ks + 2)
                    if ks >= 1:
                        k.w(v, "tr_mm", t0_ + ks * NG)
                    ev = v.tensor_copy(k.DTS[0:4, 0:T], dps[0:4, :])
                    k.inc(ev, "e_a", 1)
                    for gr in range(NG):
                        n = NG * ks + gr
                        k.w(v, "tr_mm", t0_ + n + 1)
                        ev = v.tensor_copy(
                            k.DTM[:, gr * 32 + ks * 4:gr * 32 + (ks + 1) * 4],
                            tp4[n % 2])
                        k.inc(ev, "tr_ev", 1)

            @blk.gpsimd
            def _(g):
                for gi in self._w_prefetch:
                    self._q_load(g, gi)

    def ph_branch_v(self, qb, prefetch):
        k = self
        T, NT = self.T, self.NT
        nq = DV // QW
        with self.nc.Block() as blk, ExitStack() as pstk:
            mup = [k.ps(pstk, "mupv0", [128, T], F32),
                   k.ps(pstk, "mupv1", [128, T], F32)]
            pcv = [k.ps(pstk, "pcv0", [128, QW], F32),
                   k.ps(pstk, "pcv1", [128, QW], F32),
                   k.ps(pstk, "pcv2", [128, QW], F32),
                   k.ps(pstk, "pcv3", [128, QW], F32)]
            mu0, xin0 = k.c["mu_mm"], k.c["xin_v"]
            xg0 = k.c["xin_g"]
            fe0, ev0 = k.c["fe_mm"], k.c["fe_ev"]

            @blk.tensor
            def _(t):
                t = TEng(t)
                for cc in range(NCC):
                    if cc >= 2:
                        k.w(t, "xin_v", xin0 + cc - 1)
                    mm = k.mm(t, mup[cc % 2],
                              k.WX2A[96:128, cc * 128:(cc + 1) * 128],
                              k.Z0[96:128, 0:T], start=True, stop=True,
                              tp=(96, 0))
                    k.inc(mm, "mu_mm", 1)
                k.w(t, "xin_v", xin0 + NCC)
                for qp in range(nq // 2):
                    self._q_wait(t, qb + 2 * qp)
                    self._q_wait(t, qb + 2 * qp + 1)
                    for tt in range(NT):
                        n = qp * NT + tt
                        if n >= 2:
                            k.w(t, "fe_ev", ev0 + 2 * (n - 1))
                        for cc in range(NCC):
                            lhsT = k.MUXF[:, cc * T + tt * 128:cc * T + (tt + 1) * 128]
                            for j in range(2):
                                slot = self._q_slot(qb + 2 * qp + j)
                                mm = k.mm(t, pcv[2 * (n % 2) + j],
                                          lhsT, slot[:, cc * QW:(cc + 1) * QW],
                                          start=(cc == 0), stop=(cc == NCC - 1))
                        k.inc(mm, "fe_mm", 1)
                    k.qfree[qb + 2 * qp] = ("fe_mm", k.c["fe_mm"])
                    k.qfree[qb + 2 * qp + 1] = ("fe_mm", k.c["fe_mm"])

            @blk.vector
            def _(v):
                v = Eng(v)
                for cc in range(NCC):
                    k.w(v, "mu_mm", mu0 + cc + 1)
                    k.w(v, "xin_g", xg0 + cc + 1)
                    i2 = self._xin_mu(v, cc, T, mup[cc % 2], k.MUXF, cc * T)
                    k.inc(i2, "xin_v", 1)
                for qp in range(nq // 2):
                    for tt in range(NT):
                        n = qp * NT + tt
                        k.w(v, "fe_mm", fe0 + n + 1)
                        for j in range(2):
                            vc = (2 * qp + j) * QW
                            ev = v.tensor_copy(
                                k.V[:, tt * DV + vc:tt * DV + vc + QW],
                                pcv[2 * (n % 2) + j])
                            k.inc(ev, "fe_ev", 1)

            @blk.gpsimd
            def _(g):
                ge = Eng(g)
                for cc in range(NCC):
                    self._xin_sub(ge, cc, T, xin0)
                for gi in prefetch:
                    self._q_load(g, gi)


    def _contrib_mms(self, t, ch, CP, buf=0):
        k = self
        gr, j = ch // 4, ch % 4
        last = None
        for sub in range(4):
            for hs in range(8):
                h = 4 * hs + sub
                lhsT = k.KHTGB[buf][32 * j:32 * j + 32,
                                    hs * 128 + sub * 32:
                                    hs * 128 + sub * 32 + 32]
                rhs = k.V[32 * j:32 * j + 32,
                          gr * DV + 64 * h:gr * DV + 64 * h + 64]
                last = t.matmul(CP[32 * sub:32 * sub + 32,
                                   64 * hs:64 * hs + 64],
                                lhsT, rhs, start=True, stop=True,
                                tile_position=(32 * j, 32 * sub))
        return last

    def _khtg_mms(self, t, k_, gr, tq, base_n, te0, t0_):
        for ks in range(8):
            n = base_n + ks
            if n >= 2:
                k_.w(t, "tr_ev", te0 + n - 1)
            mm = t.transpose(tq[n % 2],
                             k_.KH[:, ks * k_.T + gr * 128:
                                   ks * k_.T + (gr + 1) * 128], k_.IDENT)
            k_.inc(mm, "tr_mm", 1)

    def _khtg_evs(self, v, k_, base_n, tq, t0_, buf=0):
        for ks in range(8):
            n = base_n + ks
            k_.w(v, "tr_mm", t0_ + n + 1)
            ev = v.tensor_copy(k_.KHTGB[buf][:, ks * 128:(ks + 1) * 128],
                               tq[n % 2])
            k_.inc(ev, "tr_ev", 1)

    def _s_update(self, v, ch, CP):
        k = self
        NCH = self.NCH
        s3 = k.S.rearrange("p (g v) -> p g v", g=8)
        v.tensor_tensor(s3, s3, CP.rearrange("p (g v) -> p g v", g=8), ALU.add)
        decb = k.DEC[:, ch:8 * NCH:NCH].unsqueeze(2).broadcast_to((128, 8, 64))
        return v.tensor_tensor(s3, s3, decb, ALU.mult)

    def ph_sum(self, prefetch=()):
        k = self
        NCH, NG = self.NCH, self.NG
        with self.nc.Block() as blk, ExitStack() as pstk:
            CP = k.ps(pstk, "cps", [128, 512], F32)
            tq = [k.ps(pstk, "tqs0", [128, 128], BF16),
                  k.ps(pstk, "tqs1", [128, 128], BF16)]
            m0, u0 = k.c["sum_mm"], k.c["sum_up"]
            t0_, te0 = k.c["tr_mm"], k.c["tr_ev"]

            @blk.tensor
            def _(t):
                t = TEng(t)
                for ch in range(NCH):
                    gr, j = ch // 4, ch % 4
                    if j == 0:
                        if gr >= 1:
                            k.w(t, "sum_up", u0 + ch + 1)
                        self._khtg_mms(t, k, gr, tq, gr * 8, te0, t0_)
                    k.w(t, "sum_up", u0 + ch + 1)
                    k.w(t, "tr_ev", te0 + (gr + 1) * 8)
                    last = self._contrib_mms(t, ch, CP)
                    k.inc(last, "sum_mm", 1)

            @blk.vector
            def _(v):
                v = Eng(v)
                i = v.memset(k.S, 0.0)
                k.inc(i, "sum_up", 1)
                for ch in range(NCH):
                    gr, j = ch // 4, ch % 4
                    if j == 0:
                        self._khtg_evs(v, k, gr * 8, tq, t0_)
                    k.w(v, "sum_mm", m0 + ch + 1)
                    i = self._s_update(v, ch, CP)
                    k.inc(i, "sum_up", 1)
                v.tensor_copy(k.SB16B[1], k.S)

            @blk.gpsimd
            def _(g):
                for gi in prefetch:
                    self._q_load(g, gi)

    def ph_p(self):
        k = self
        T, NG = self.T, self.NG
        with self.nc.Block() as blk, ExitStack() as pstk:
            PP = [k.ps(pstk, "pp0", [128, 1024], F32),
                  k.ps(pstk, "pp1", [128, 1024], F32)]
            p0, e0 = k.c["p_mm"], k.c["pm_ev"]

            @blk.tensor
            def _(t):
                t = TEng(t)
                for gr in range(NG):
                    if gr >= 2:
                        k.w(t, "pm_ev", e0 + 8 * (gr - 1))
                    for sub in range(4):
                        for j in range(4):
                            ch = gr * 4 + j
                            for hs in range(8):
                                lhsT = k.KH[32 * sub:32 * sub + 32,
                                            hs * T + ch * CH:
                                            hs * T + (ch + 1) * CH]
                                rhs = k.QT[32 * sub:32 * sub + 32,
                                           hs * T + ch * CH:
                                           hs * T + (ch + 1) * CH]
                                last = t.matmul(
                                    PP[gr % 2][32 * j:32 * j + 32,
                                               hs * 128 + 32 * sub:
                                               hs * 128 + 32 * sub + 32],
                                    lhsT, rhs, start=True, stop=True,
                                    tile_position=(32 * sub, 32 * j))
                    k.inc(last, "p_mm", 1)

            @blk.vector
            def _(v):
                v = Eng(v)
                for gr in range(NG):
                    k.w(v, "p_mm", p0 + gr + 1)
                    for hs in range(8):
                        ev = v.tensor_tensor(
                            k.PM[:, gr * 1024 + hs * 128:
                                 gr * 1024 + (hs + 1) * 128],
                            PP[gr % 2][:, hs * 128:(hs + 1) * 128],
                            k.MASKP, ALU.mult)
                        k.inc(ev, "pm_ev", 1)

    def ph_chain(self):
        k = self
        T, NG, NCH = self.T, self.NG, self.NCH
        with self.nc.Block() as blk, ExitStack() as pstk:
            OP = k.ps(pstk, "opc", [128, 2048], F32)
            CPb = [k.ps(pstk, "cpc0", [128, 512], F32),
                   k.ps(pstk, "cpc1", [128, 512], F32)]
            tq = [k.ps(pstk, "tqc0", [128, 128], BF16),
                  k.ps(pstk, "tqc1", [128, 128], BF16)]
            m0, u0, e0 = k.c["c_mm"], k.c["c_up"], k.c["c_ev"]
            ci0 = k.c["c_oi"]
            s0g, v0g = k.c["h_s"], k.c["gn_v"]
            t0_, te0 = k.c["tr_mm"], k.c["tr_ev"]

            @blk.tensor
            def _(t):
                t = TEng(t)

                def contrib_part(x):
                    g = x // 4
                    if x % 4 == 0:
                        self._khtg_mms(t, k, g, tq, g * 8, te0, t0_)
                        k.w(t, "tr_ev", te0 + (g + 1) * 8)
                    if x >= 2:
                        k.w(t, "c_up", u0 + x - 1)
                    last = self._contrib_mms(t, x, CPb[x % 2], buf=g % 2)
                    k.inc(last, "c_mm", 1)

                contrib_part(0)
                for ch in range(NCH):
                    gr, j = ch // 4, ch % 4
                    if ch + 1 < NCH:
                        contrib_part(ch + 1)
                    if j == 0 and gr >= 1:
                        k.w(t, "c_ev", e0 + gr)
                    k.w(t, "c_up", u0 + ch)
                    last_i = None
                    for sub in range(4):
                        for hs in range(8):
                            h = 4 * hs + sub
                            lhsT = k.QT[32 * sub:32 * sub + 32,
                                        hs * T + ch * CH:hs * T + (ch + 1) * CH]
                            rhs = k.SB16B[(ch + 1) % 2][
                                32 * sub:32 * sub + 32, 64 * hs:64 * hs + 64]
                            last_i = t.matmul(OP[32 * j:32 * j + 32,
                                              64 * h:64 * h + 64],
                                              lhsT, rhs, start=(h % 8 == 0),
                                              stop=False,
                                              tile_position=(32 * sub, 32 * j))
                    k.inc(last_i, "c_oi", 1)
                    for h in range(NH):
                        hs, sub = h // 4, h % 4
                        lhsT = k.PM[32 * j:32 * j + 32,
                                    gr * 1024 + hs * 128 + sub * 32:
                                    gr * 1024 + hs * 128 + sub * 32 + 32]
                        rhs = k.V[32 * j:32 * j + 32,
                                  gr * DV + 64 * h:gr * DV + 64 * h + 64]
                        last_i = t.matmul(OP[32 * j:32 * j + 32,
                                           64 * h:64 * h + 64],
                                          lhsT, rhs, start=False,
                                          stop=(h % 8 == 7),
                                          tile_position=(32 * j, 32 * j))
                    k.inc(last_i, "c_oi", 1)

            @blk.scalar
            def _(sc):
                sc = Eng(sc)
                NGm = NCH // 4
                for gr in range(NGm):
                    if gr >= 1:
                        k.w(sc, "gn_v", v0g + 3 * (gr - 1) + 2)
                    k.w(sc, "c_ev", e0 + gr + 1)
                    a = sc.activation(k.SQC, k.Og[gr], AF.Square)
                    k.inc(a, "h_s", 1)
                    k.w(sc, "gn_v", v0g + 3 * gr + 2)
                    a = sc.activation(k.GNS[:, 64:96], k.GNS[:, 96:128],
                                      AF.Sqrt)
                    k.inc(a, "h_s", 1)

            @blk.vector
            def _(v):
                v = Eng(v)
                for ch in range(NCH):
                    gr, j = ch // 4, ch % 4
                    if ch == 0:
                        self._khtg_evs(v, k, 0, tq, t0_, buf=0)
                    if j == 3 and ch + 1 < NCH:
                        g2 = (ch + 1) // 4
                        self._khtg_evs(v, k, g2 * 8, tq, t0_, buf=g2 % 2)
                    k.w(v, "c_mm", m0 + ch + 1)
                    if ch >= 1:
                        k.w(v, "c_oi", ci0 + 2 * (ch - 1) + 1)
                    self._s_update(v, ch, CPb[ch % 2])
                    i = v.tensor_copy(k.SB16B[ch % 2], k.S)
                    k.inc(i, "c_up", 1)
                    if j == 3:
                        k.w(v, "c_oi", ci0 + 2 * ch + 2)
                        osl = k.Og[gr]
                        v.tensor_copy(osl, OP)
                        for h in range(NH):
                            i = v.scalar_tensor_tensor(
                                osl[:, 64 * h:64 * h + 64],
                                k.V[:, gr * DV + 64 * h:gr * DV + 64 * h + 64],
                                k.DTM[:, gr * 32 + h:gr * 32 + h + 1],
                                osl[:, 64 * h:64 * h + 64], ALU.mult, ALU.add)
                        k.inc(i, "c_ev", 1)
                        o3 = osl.rearrange("p (h v) -> p h v", h=NH)
                        v.tensor_reduce(k.GNS[:, 0:32].unsqueeze(2), o3,
                                        AX.X, ALU.add)
                        i = v.tensor_scalar(k.GNS[:, 0:32], k.GNS[:, 0:32],
                                            1.0 / 64, None, ALU.mult)
                        k.inc(i, "gn_v", 1)
                        k.w(v, "h_s", s0g + 2 * gr + 1)
                        v.tensor_reduce(k.GNS[:, 32:64].unsqueeze(2),
                                        k.SQC.rearrange("p (h v) -> p h v",
                                                        h=NH),
                                        AX.X, ALU.add)
                        v.tensor_scalar(k.GNS[:, 32:64], k.GNS[:, 32:64],
                                        1.0 / 64, EPS, ALU.mult, ALU.add)
                        v.tensor_tensor(k.GNS[:, 96:128], k.GNS[:, 0:32],
                                        k.GNS[:, 0:32], ALU.mult)
                        v.tensor_tensor(k.GNS[:, 96:128], k.GNS[:, 32:64],
                                        k.GNS[:, 96:128], ALU.subtract)
                        i = v.reciprocal(k.GNS[:, 96:128], k.GNS[:, 96:128])
                        k.inc(i, "gn_v", 1)
                        k.w(v, "h_s", s0g + 2 * gr + 2)
                        for h in range(NH):
                            v.tensor_scalar(osl[:, 64 * h:64 * h + 64],
                                            osl[:, 64 * h:64 * h + 64],
                                            k.GNS[:, h:h + 1],
                                            k.GNS[:, 64 + h:64 + h + 1],
                                            ALU.subtract, ALU.mult)
                        v.tensor_tensor(osl, osl, k.GNW, ALU.mult)
                        i = v.tensor_tensor(osl, osl, k.GNB, ALU.add)
                        k.inc(i, "gn_v", 1)

    def ph_gnconst(self):
        k = self
        with self.nc.Block() as blk:
            @blk.sync
            def _(s):
                n1 = k.inc(s.dma_start(out=k.GNW,
                                       in_=k.D["gnw_row"].broadcast_to((128, DV))),
                           "cst", 16)
                n2 = k.inc(s.dma_start(out=k.GNB,
                                       in_=k.D["gnb_row"].broadcast_to((128, DV))),
                           "cst", 16)
                k.w(s, "cst", n2)

    def ph_gn(self):
        k = self
        T, NT, NG = self.T, self.NT, self.NG
        NGH = max(1, NG // 2)
        with self.nc.Block() as blk:
            s0, v0 = k.c["h_s"], k.c["gn_v"]

            @blk.scalar
            def _(sc):
                sc = Eng(sc)
                for tt in range(NT):
                    osl = k.Og[tt]
                    if tt >= 1:
                        k.w(sc, "gn_v", v0 + 3 * (tt - 1) + 2)
                    a = sc.activation(k.SQB, osl, AF.Square)
                    k.inc(a, "h_s", 1)
                    k.w(sc, "gn_v", v0 + 3 * tt + 2)
                    a = sc.activation(k.GNS[:, 64:96], k.GNS[:, 96:128], AF.Sqrt)
                    k.inc(a, "h_s", 1)

            @blk.vector
            def _(v):
                v = Eng(v)
                for tt in range(NT):
                    osl = k.Og[tt]
                    o3 = osl.rearrange("p (h v) -> p h v", h=NH)
                    v.tensor_reduce(k.GNS[:, 0:32].unsqueeze(2), o3, AX.X, ALU.add)
                    i = v.tensor_scalar(k.GNS[:, 0:32], k.GNS[:, 0:32],
                                        1.0 / 64, None, ALU.mult)
                    k.inc(i, "gn_v", 1)
                    k.w(v, "h_s", s0 + 2 * tt + 1)
                    v.tensor_reduce(k.GNS[:, 32:64].unsqueeze(2),
                                    k.SQB.rearrange("p (h v) -> p h v", h=NH),
                                    AX.X, ALU.add)
                    v.tensor_scalar(k.GNS[:, 32:64], k.GNS[:, 32:64],
                                    1.0 / 64, EPS, ALU.mult, ALU.add)
                    v.tensor_tensor(k.GNS[:, 96:128], k.GNS[:, 0:32],
                                    k.GNS[:, 0:32], ALU.mult)
                    v.tensor_tensor(k.GNS[:, 96:128], k.GNS[:, 32:64],
                                    k.GNS[:, 96:128], ALU.subtract)
                    i = v.reciprocal(k.GNS[:, 96:128], k.GNS[:, 96:128])
                    k.inc(i, "gn_v", 1)
                    k.w(v, "h_s", s0 + 2 * tt + 2)
                    on = osl
                    for h in range(NH):
                        v.tensor_scalar(on[:, 64 * h:64 * h + 64],
                                        osl[:, 64 * h:64 * h + 64],
                                        k.GNS[:, h:h + 1],
                                        k.GNS[:, 64 + h:64 + h + 1],
                                        ALU.subtract, ALU.mult)
                    v.tensor_tensor(on, on, k.GNW, ALU.mult)
                    i = v.tensor_tensor(on, on, k.GNB, ALU.add)
                    k.inc(i, "gn_v", 1)

    def ph_trans(self, src, dst, nvc):
        k = self
        T, NT = self.T, self.NT
        with self.nc.Block() as blk, ExitStack() as pstk:
            tps = [k.ps(pstk, "tpt0", [128, 128], BF16),
                   k.ps(pstk, "tpt1", [128, 128], BF16)]
            t0_, e0 = k.c["tr_mm"], k.c["tr_ev"]

            @blk.tensor
            def _(t):
                t = TEng(t)
                n = 0
                for tt in range(NT):
                    for vc in range(nvc):
                        if n >= 2:
                            k.w(t, "tr_ev", e0 + n - 1)
                        mm = t.transpose(
                            tps[n % 2],
                            src[:, tt * (nvc * 128) + vc * 128:
                                tt * (nvc * 128) + (vc + 1) * 128], k.IDENT)
                        k.inc(mm, "tr_mm", 1)
                        n += 1

            @blk.vector
            def _(v):
                v = Eng(v)
                n = 0
                for tt in range(NT):
                    for vc in range(nvc):
                        k.w(v, "tr_mm", t0_ + n + 1)
                        ev = v.tensor_copy(
                            dst[:, vc * T + tt * 128:vc * T + (tt + 1) * 128],
                            tps[n % 2])
                        k.inc(ev, "tr_ev", 1)
                        n += 1

    def ph_lateg_xin(self):
        k = self
        T = self.T
        with self.nc.Block() as blk, ExitStack() as pstk:
            mup = [k.ps(pstk, "mupg0", [128, T], F32),
                   k.ps(pstk, "mupg1", [128, T], F32)]
            mu0, xin0 = k.c["mu_mm"], k.c["xin_v"]

            @blk.tensor
            def _(t):
                t = TEng(t)
                for cc in range(NCC):
                    if cc >= 2:
                        k.w(t, "xin_v", xin0 + cc - 1)
                    mm = k.mm(t, mup[cc % 2],
                              k.WX2B[0:32, cc * 128:(cc + 1) * 128],
                              k.Z1[0:32, 0:T], start=True, stop=True, tp=(0, 0))
                    k.inc(mm, "mu_mm", 1)

            @blk.vector
            def _(v):
                v = Eng(v)
                for cc in range(NCC):
                    k.w(v, "mu_mm", mu0 + cc + 1)
                    i2 = self._xin_dve(v, cc, 0, T, mup[cc % 2], k.XIN, cc * T)
                    k.inc(i2, "xin_v", 1)

    def ph_lateg_mm(self):
        k = self
        T = self.T
        with self.nc.Block() as blk, ExitStack() as pstk:
            gp = [k.ps(pstk, "gp0", [128, T], F32),
                  k.ps(pstk, "gp1", [128, T], F32)]
            m0, s0, v0 = k.c["lg_mm"], k.c["lg_s"], k.c["lg_v"]
            gq = [0, 0]
            gqv = []

            @blk.gpsimd
            def _(g):
                for gs in range(16):
                    if gs >= 2:
                        k.w(g, "lg_mm", m0 + gs - 1)
                    v_ = k.inc(g.dma_start(out=k.WGSL[gs % 2],
                                           in_=k.D["wgs"][gs]), f"gq{gs % 2}", 16)
                    gqv.append(v_)
                for dc in range(10):
                    k.inc(g.dma_start(out=k._wo_q(dc), in_=k.D["wos"][dc]),
                          "wor", 16)

            @blk.tensor
            def _(t):
                t = TEng(t)
                for gs in range(16):
                    k.w(t, f"gq{gs % 2}", gqv[gs])
                    if gs >= 2:
                        k.w(t, "lg_s", s0 + gs - 1)
                    for cc in range(NCC):
                        last = k.mm(t, gp[gs % 2],
                                    k.WGSL[gs % 2][:, cc * 128:(cc + 1) * 128],
                                    k.XIN[:, cc * T:(cc + 1) * T],
                                    start=(cc == 0), stop=(cc == NCC - 1))
                    k.inc(last, "lg_mm", 1)

            @blk.scalar
            def _(sc):
                sc = Eng(sc)
                for gs in range(16):
                    k.w(sc, "lg_mm", m0 + gs + 1)
                    if gs >= 1:
                        k.w(sc, "lg_v", v0 + gs)
                    a = sc.activation(k.SILS[:, 0:T], gp[gs % 2], AF.Silu)
                    k.inc(a, "lg_s", 1)

            @blk.vector
            def _(v):
                v = Eng(v)
                for gs in range(16):
                    k.w(v, "lg_s", s0 + gs + 1)
                    sl = k.ONFM[:, gs * T:(gs + 1) * T]
                    i = v.tensor_tensor(sl, sl, k.SILS[:, 0:T], ALU.mult)
                    k.inc(i, "lg_v", 1)

    def ph_wo(self):
        k = self
        T, NT = self.T, self.NT
        Vf = self.V.bitcast(F32)
        WOC = Vf[:, 4096:4096 + HID]
        with self.nc.Block() as blk, ExitStack() as pstk:
            WOP = k.ps(pstk, "wop", [128, 2048], F32)
            tpx = [k.ps(pstk, "tpx0", [128, 128], BF16),
                   k.ps(pstk, "tpx1", [128, 128], BF16)]
            xldv = []
            X1T = self.X1T
            wo0, wv0 = k.c["wo_mm"], k.c["wo_v"]
            s0 = k.c["s_a"]
            t0_, e0 = k.c["tr_mm"], k.c["tr_ev"]
            x10 = k.c["x1s"]
            wr0 = k.c["wor"]
            q00 = k.c["woq0"]

            @blk.gpsimd
            def _(g):
                for dc in range(10, 16):
                    k.inc(g.dma_start(out=k._wo_q(dc), in_=k.D["wos"][dc]),
                          "woq0", 16)

            @blk.sync
            def _(s):
                for tt in range(NT):
                    if tt >= 2:
                        k.w(s, "wo_v", wv0 + 5 * (tt - 1))
                    v_ = k.inc(s.dma_start(
                        out=k.XT[tt % 2],
                        in_=k.D["xsl"][1 + LB + tt * 128:1 + LB + (tt + 1) * 128, :]),
                        f"xld{tt % 2}", 16)
                    xldv.append(v_)
                    k.w(s, "wo_v", wv0 + 5 * tt + 3)
                    k.inc(s.dma_start(out=k.D["x1d"][tt * 128:(tt + 1) * 128, :],
                                      in_=X1T), "x1s", 16)
                k.w(s, "x1s", k.c["x1s"])

            @blk.tensor
            def _(t):
                t = TEng(t)
                for tt in range(NT):
                    if tt >= 1:
                        k.w(t, "wo_v", wv0 + 5 * (tt - 1) + 1)
                    for dc in range(NCC):
                        if tt == 0 and dc == 0:
                            k.w(t, "wor", wr0)
                        if tt == 0 and dc == 10:
                            k.w(t, "woq0", q00 + 96)
                        lhsT = k.ONFM[:, dc * T + tt * 128:dc * T + (tt + 1) * 128]
                        for q in range(4):
                            last = t.matmul(WOP[:, q * 512:(q + 1) * 512],
                                            lhsT,
                                            k._wo_q(dc)[:, q * 512:(q + 1) * 512],
                                            start=(dc == 0), stop=(dc == NCC - 1))
                        k.inc(last, "wo_mm", 1)
                    if tt >= 1:
                        k.w(t, "wo_v", wv0 + 5 * (tt - 1) + 5)
                        for cc in range(NCC):
                            n = (tt - 1) * NCC + cc
                            if n >= 2:
                                k.w(t, "tr_ev", e0 + n - 1)
                            mm = t.transpose(tpx[n % 2],
                                             k.SCR[0][:, cc * 128:(cc + 1) * 128],
                                             k.IDENT)
                            k.inc(mm, "tr_mm", 1)
                k.w(t, "wo_v", wv0 + 5 * (NT - 1) + 5)
                for cc in range(NCC):
                    n = (NT - 1) * NCC + cc
                    if n >= 2:
                        k.w(t, "tr_ev", e0 + n - 1)
                    mm = t.transpose(tpx[n % 2],
                                     k.SCR[0][:, cc * 128:(cc + 1) * 128],
                                     k.IDENT)
                    k.inc(mm, "tr_mm", 1)

            @blk.scalar
            def _(sc):
                sc = Eng(sc)
                for tt in range(NT):
                    k.w(sc, "wo_v", wv0 + 5 * tt + 1)
                    a = sc.activation(k.SCF, WOC, AF.Square,
                                      accum_out=k.ST1[:, 4:5])
                    k.inc(a, "s_a", 1)
                    k.w(sc, "wo_v", wv0 + 5 * tt + 2)
                    a = sc.activation(k.ST1[:, 5:6], k.ST1[:, 6:7], AF.Sqrt)
                    k.inc(a, "s_a", 1)
                    k.w(sc, "wo_v", wv0 + 5 * tt + 3)
                    a = sc.activation(k.SCF, X1T, AF.Square,
                                      accum_out=k.ST1[:, 12:13])
                    k.inc(a, "s_a", 1)
                    k.w(sc, "wo_v", wv0 + 5 * tt + 4)
                    a = sc.activation(k.ST1[:, 13:14], k.ST1[:, 14:15], AF.Sqrt)
                    k.inc(a, "s_a", 1)

            @blk.vector
            def _(v):
                v = Eng(v)
                for tt in range(NT):
                    k.w(v, "wo_mm", wo0 + (tt + 1) * NCC)
                    i = v.tensor_copy(WOC, WOP)
                    k.inc(i, "wo_v", 1)
                    v.tensor_reduce(k.ST1[:, 0:1], WOC, AX.X, ALU.add)
                    v.tensor_scalar(k.ST1[:, 1:2], k.ST1[:, 0:1], 1.0 / HID,
                                    None, ALU.mult)
                    k.w(v, "s_a", s0 + 4 * tt + 1)
                    v.tensor_scalar(k.ST1[:, 2:3], k.ST1[:, 4:5], 1.0 / HID,
                                    EPS, ALU.mult, ALU.add)
                    v.tensor_tensor(k.ST1[:, 3:4], k.ST1[:, 1:2], k.ST1[:, 1:2],
                                    ALU.mult)
                    v.tensor_tensor(k.ST1[:, 2:3], k.ST1[:, 2:3], k.ST1[:, 3:4],
                                    ALU.subtract)
                    i = v.reciprocal(k.ST1[:, 6:7], k.ST1[:, 2:3])
                    k.inc(i, "wo_v", 1)
                    k.w(v, "s_a", s0 + 4 * tt + 2)
                    k.w(v, f"xld{tt % 2}", xldv[tt])
                    v.tensor_scalar(k.SCF, WOC, k.ST1[:, 1:2], k.ST1[:, 5:6],
                                    ALU.subtract, ALU.mult)
                    if tt >= 1:
                        k.w(v, "x1s", x10 + 16 * tt)
                    i = v.tensor_tensor(X1T, k.SCF, k.XT[tt % 2], ALU.add)
                    k.inc(i, "wo_v", 1)
                    v.tensor_reduce(k.ST1[:, 8:9], X1T, AX.X, ALU.add)
                    v.tensor_scalar(k.ST1[:, 9:10], k.ST1[:, 8:9], 1.0 / HID,
                                    None, ALU.mult)
                    k.w(v, "s_a", s0 + 4 * tt + 3)
                    v.tensor_scalar(k.ST1[:, 10:11], k.ST1[:, 12:13], 1.0 / HID,
                                    EPS, ALU.mult, ALU.add)
                    v.tensor_tensor(k.ST1[:, 11:12], k.ST1[:, 9:10],
                                    k.ST1[:, 9:10], ALU.mult)
                    v.tensor_tensor(k.ST1[:, 10:11], k.ST1[:, 10:11],
                                    k.ST1[:, 11:12], ALU.subtract)
                    i = v.reciprocal(k.ST1[:, 14:15], k.ST1[:, 10:11])
                    k.inc(i, "wo_v", 1)
                    k.w(v, "s_a", s0 + 4 * tt + 4)
                    if tt >= 1:
                        for cc in range(NCC):
                            n = (tt - 1) * NCC + cc
                            k.w(v, "tr_mm", t0_ + n + 1)
                            ev = v.tensor_copy(
                                k.XFF[:, cc * T + (tt - 1) * 128:
                                      cc * T + tt * 128],
                                tpx[n % 2])
                            k.inc(ev, "tr_ev", 1)
                        k.w(v, "tr_mm", t0_ + tt * NCC)
                    i = v.tensor_scalar(k.SCR[0], X1T, k.ST1[:, 9:10],
                                        k.ST1[:, 13:14], ALU.subtract, ALU.mult)
                    k.inc(i, "wo_v", 1)
                for cc in range(NCC):
                    n = (NT - 1) * NCC + cc
                    k.w(v, "tr_mm", t0_ + n + 1)
                    ev = v.tensor_copy(
                        k.XFF[:, cc * T + (NT - 1) * 128:cc * T + NT * 128],
                        tpx[n % 2])
                    k.inc(ev, "tr_ev", 1)

    def ph_router(self):
        k = self
        T, NT = self.T, self.NT
        BIG = 65536.0
        with self.nc.Block() as blk, ExitStack() as pstk:
            RTP = [k.ps(pstk, "rtp0", [128, 16], F32),
                   k.ps(pstk, "rtp1", [128, 16], F32)]
            SGP = [k.ps(pstk, "sgp0", [128, 8], F32),
                   k.ps(pstk, "sgp1", [128, 8], F32)]
            m0, v0, s0 = k.c["rt_mm"], k.c["rt_v"], k.c["h_s"]

            @blk.tensor
            def _(t):
                t = TEng(t)
                for tt in range(NT):
                    if tt >= 2:
                        k.w(t, "rt_v", v0 + 2 * (tt - 1))
                    for cc in range(NCC):
                        lhsT = k.XFF[:, cc * T + tt * 128:cc * T + (tt + 1) * 128]
                        t.matmul(RTP[tt % 2][:, 0:NE], lhsT,
                                 k.GWSB[:, cc * NE:(cc + 1) * NE],
                                 start=(cc == 0), stop=(cc == NCC - 1))
                        last = t.matmul(SGP[tt % 2][:, 0:1], lhsT,
                                        k.SGWB[:, cc:cc + 1],
                                        start=(cc == 0), stop=(cc == NCC - 1))
                    k.inc(last, "rt_mm", 1)

            @blk.scalar
            def _(sc):
                sc = Eng(sc)
                for tt in range(NT):
                    k.w(sc, "rt_v", v0 + 2 * tt + 1)
                    a = sc.activation(k.PR8[:, 0:8], RTP[tt % 2][:, 0:NE],
                                      AF.Exp, bias=k.PR8[:, 8:9])
                    k.inc(a, "h_s", 1)
                    a = sc.activation(k.SGC[:, tt:tt + 1], SGP[tt % 2][:, 0:1],
                                      AF.Sigmoid)
                    k.inc(a, "h_s", 1)

            @blk.vector
            def _(v):
                v = Eng(v)
                for tt in range(NT):
                    k.w(v, "rt_mm", m0 + tt + 1)
                    v.reduce_max(k.PR8[:, 8:9], RTP[tt % 2][:, 0:NE], AX.X)
                    i = v.tensor_scalar(k.PR8[:, 8:9], k.PR8[:, 8:9], -1.0,
                                        None, ALU.mult)
                    k.inc(i, "rt_v", 1)
                    k.w(v, "h_s", s0 + 2 * tt + 1)
                    v.tensor_reduce(k.PR8[:, 9:10], k.PR8[:, 0:8], AX.X, ALU.add)
                    v.reciprocal(k.PR8[:, 9:10], k.PR8[:, 9:10])
                    v.tensor_scalar(k.PR8[:, 0:8], k.PR8[:, 0:8],
                                    k.PR8[:, 9:10], None, ALU.mult)
                    v.reduce_max(k.PR8[:, 10:11], k.PR8[:, 0:8], AX.X)
                    v.tensor_scalar(k.STW, k.PR8[:, 0:8], k.PR8[:, 10:11],
                                    None, ALU.is_ge)
                    v.scalar_tensor_tensor(k.STW, k.STW, -BIG, k.PR8[:, 0:8],
                                           ALU.mult, ALU.add)
                    v.reduce_max(k.PR8[:, 11:12], k.STW, AX.X)
                    v.tensor_scalar(k.STW, k.PR8[:, 0:8], k.PR8[:, 11:12],
                                    None, ALU.is_ge)
                    i = v.tensor_tensor(k.CMB[:, tt * 8:(tt + 1) * 8],
                                        k.PR8[:, 0:8], k.STW, ALU.mult)
                    k.inc(i, "rt_v", 1)

    def _wpj_regions(self):
        T = self.T
        regs = []
        regs.append((self.XNT, 0, (NCC * (T + 1)) // HID))
        regs.append((self.arB, 0, 4))
        vstart = ((NFC * T + HID - 1) // HID) * HID
        regs.append((self.V, vstart, (self.AW - vstart) // HID))
        out = []
        left = NFC
        for ten, o0, n in regs:
            take = min(left, n)
            if take > 0:
                out.append((ten, o0, take))
            left -= take
        assert left == 0, "wpj capacity"
        return out

    def _wpj_spans(self):
        out = []
        fc0 = 0
        for ten, o0, n in self._wpj_regions():
            out.append((ten[:, o0:o0 + n * HID], fc0, n))
            fc0 += n
        return out

    def _wpj(self, fc):
        i = fc
        for ten, o0, n in self._wpj_regions():
            if i < n:
                return ten[:, o0 + i * HID:o0 + (i + 1) * HID]
            i -= n
        raise IndexError(fc)

    def ph_experts(self):
        k = self
        T, NT = self.T, self.NT
        NEX = NE + 4
        with self.nc.Block() as blk, ExitStack() as pstk:
            HPA = k.ps(pstk, "hpa", [128, 2048], F32)
            PA0 = k.ps(pstk, "pa", [128, 2048], F32)
            HP1 = [HPA[:, 0:T], PA0[:, 0:T]]
            HP2 = [HPA[:, 1024:1024 + T], PA0[:, 1024:1024 + T]]
            PA = [PA0, HPA]
            hm0, hs0, hv0 = k.c["h_mm"], k.c["h_s"], k.c["h_v"]
            ym0, yv0 = k.c["y_mm"], k.c["y_v"]
            fqv, f2qv, pqv = [], [], []

            def wsrc(e):
                if e < NE:
                    return (lambda fc: k.D["fc1s"][e, fc],
                            lambda fc: k.D["fc2s"][e, fc],
                            lambda fc0, nfc: k.D["pjs"][e, fc0:fc0 + nfc])
                st = e - NE
                return (lambda fc: k.D["sf1s"][st * NFC + fc],
                        lambda fc: k.D["sf2s"][st * NFC + fc],
                        lambda fc0, nfc: k.D["spjs"][st * NFC + fc0:
                                                     st * NFC + fc0 + nfc])

            @blk.gpsimd
            def _(g):
                for e in range(NEX):
                    f1, f2, pjd = wsrc(e)
                    for fc in range(NFC):
                        n = e * NFC + fc
                        if n >= 4:
                            k.w(g, "h_mm", hm0 + n - 3)
                        v1 = k.inc(g.dma_start(out=k.FC1SL[n % 4], in_=f1(fc)),
                                   f"fq{n % 4}", 16)
                        v2 = k.inc(g.dma_start(out=k.FC2SL[n % 4], in_=f2(fc)),
                                   f"f2q{n % 4}", 16)
                        fqv.append(v1)
                        f2qv.append(v2)
                        if fc == (0 if e == 0 else 4):
                            if e >= 1:
                                k.w(g, "y_mm", ym0 + e * NT)
                            vs = [None] * 4
                            for fc_ in range(NFC):
                                si = fc_ % 4
                                vs[si] = k.inc(
                                    g.dma_start(
                                        out=self._wpj(fc_),
                                        in_=pjd(fc_, 1).rearrange(
                                            "f p c -> p (f c)")),
                                    f"pq{si}", 16)
                            pqv.append(vs)

            @blk.tensor
            def _(t):
                t = TEng(t)
                for e in range(NEX):
                    if e >= 1:
                        k.w(t, "y_v", yv0 + e * NT)
                    for fc in range(NFC):
                        n = e * NFC + fc
                        k.w(t, f"fq{n % 4}", fqv[n])
                        k.w(t, f"f2q{n % 4}", f2qv[n])
                        if n >= 2:
                            k.w(t, "h_v", hv0 + n - 1)
                        for cc in range(NCC):
                            k.mm(t, HP1[n % 2],
                                 k.FC1SL[n % 4][:, cc * 128:(cc + 1) * 128],
                                 k.XFF[:, cc * T:(cc + 1) * T],
                                 start=(cc == 0), stop=(cc == NCC - 1))
                        last = None
                        for cc in range(NCC):
                            last = k.mm(t, HP2[n % 2],
                                        k.FC2SL[n % 4][:, cc * 128:(cc + 1) * 128],
                                        k.XFF[:, cc * T:(cc + 1) * T],
                                        start=(cc == 0), stop=(cc == NCC - 1))
                        k.inc(last, "h_mm", 1)
                    for tt in range(NT):
                        n2 = e * NT + tt
                        k.w(t, "h_v", hv0 + (e + 1) * NFC)
                        if n2 >= 2:
                            k.w(t, "y_v", yv0 + n2 - 1)
                        if tt == 0:
                            for si, v_ in enumerate(pqv[e]):
                                k.w(t, f"pq{si}", v_)
                        for fc in range(NFC):
                            lhsT = k.Ht[:, fc * T + tt * 128:
                                        fc * T + (tt + 1) * 128]
                            for q in range(4):
                                last = t.matmul(
                                    PA[n2 % 2][:, q * 512:(q + 1) * 512],
                                    lhsT, self._wpj(fc)[:, q * 512:(q + 1) * 512],
                                    start=(fc == 0), stop=(fc == NFC - 1))
                        k.inc(last, "y_mm", 1)

            @blk.scalar
            def _(sc):
                sc = Eng(sc)
                for e in range(NEX):
                    for fc in range(NFC):
                        n = e * NFC + fc
                        k.w(sc, "h_mm", hm0 + n + 1)
                        if n >= 1:
                            k.w(sc, "h_v", hv0 + n)
                        a = sc.activation(k.SILS[:, 0:T], HP1[n % 2], AF.Silu)
                        k.inc(a, "h_s", 1)

            @blk.vector
            def _(v):
                v = Eng(v)
                for e in range(NEX):
                    for fc in range(NFC):
                        n = e * NFC + fc
                        k.w(v, "h_s", hs0 + n + 1)
                        k.w(v, "h_mm", hm0 + n + 1)
                        i = v.tensor_tensor(k.Ht[:, fc * T:(fc + 1) * T],
                                            k.SILS[:, 0:T], HP2[n % 2],
                                            ALU.mult)
                        k.inc(i, "h_v", 1)
                    for tt in range(NT):
                        n2 = e * NT + tt
                        k.w(v, "y_mm", ym0 + n2 + 1)
                        scal = (k.CMB[:, tt * 8 + e:tt * 8 + e + 1] if e < NE
                                else k.SGC[:, tt:tt + 1])
                        pa_ = PA[n2 % 2]
                        if e == 0:
                            i = v.tensor_scalar(k.Yt[tt], pa_, scal, None,
                                                ALU.mult)
                        else:
                            i = v.scalar_tensor_tensor(k.Yt[tt], pa_, scal,
                                                       k.Yt[tt], ALU.mult,
                                                       ALU.add)
                        k.inc(i, "y_v", 1)

    def ph_out(self):
        k = self
        T, NT = self.T, self.NT
        with self.nc.Block() as blk:
            x1v = []
            odv = []
            s0 = k.c["s_a"]
            v0 = k.c["wo_v"]
            X1L = [self.arCf[:, 0:HID], self.arCf[:, HID:2 * HID]]
            OTS = [self.arCf[:, 2 * HID:3 * HID], self.arCf[:, 3 * HID:4 * HID]]
            TMP3 = self.arBf[:, 2 * HID:3 * HID]

            @blk.sync
            def _(s):
                for tt in range(NT):
                    if tt >= 2:
                        k.w(s, "wo_v", v0 + 2 * (tt - 1))
                    v_ = k.inc(s.dma_start(
                        out=X1L[tt % 2],
                        in_=k.D["x1d"][tt * 128:(tt + 1) * 128, :]),
                        f"xld{tt % 2}", 16)
                    x1v.append(v_)
                    k.w(s, "wo_v", v0 + 2 * tt + 2)
                    o_ = k.inc(s.dma_start(out=k.out[tt * 128:(tt + 1) * 128, :],
                                           in_=OTS[tt % 2]), "o_dma", 16)
                    odv.append(o_)

            @blk.scalar
            def _(sc):
                sc = Eng(sc)
                for tt in range(NT):
                    a = sc.activation(k.SQB, k.Yt[tt], AF.Square,
                                      accum_out=k.ST1[:, 4:5])
                    k.inc(a, "s_a", 1)
                    k.w(sc, "wo_v", v0 + 2 * tt + 1)
                    a = sc.activation(k.ST1[:, 5:6], k.ST1[:, 6:7], AF.Sqrt)
                    k.inc(a, "s_a", 1)

            @blk.vector
            def _(v):
                v = Eng(v)
                for tt in range(NT):
                    yt = k.Yt[tt]
                    v.tensor_reduce(k.ST1[:, 0:1], yt, AX.X, ALU.add)
                    v.tensor_scalar(k.ST1[:, 1:2], k.ST1[:, 0:1], 1.0 / HID,
                                    None, ALU.mult)
                    k.w(v, "s_a", s0 + 2 * tt + 1)
                    v.tensor_scalar(k.ST1[:, 2:3], k.ST1[:, 4:5], 1.0 / HID,
                                    EPS, ALU.mult, ALU.add)
                    v.tensor_tensor(k.ST1[:, 3:4], k.ST1[:, 1:2], k.ST1[:, 1:2],
                                    ALU.mult)
                    v.tensor_tensor(k.ST1[:, 2:3], k.ST1[:, 2:3], k.ST1[:, 3:4],
                                    ALU.subtract)
                    i = v.reciprocal(k.ST1[:, 6:7], k.ST1[:, 2:3])
                    k.inc(i, "wo_v", 1)
                    k.w(v, "s_a", s0 + 2 * tt + 2)
                    k.w(v, f"xld{tt % 2}", x1v[tt])
                    if tt >= 2:
                        k.w(v, "o_dma", odv[tt - 2])
                    v.tensor_scalar(TMP3, yt, k.ST1[:, 1:2], k.ST1[:, 5:6],
                                    ALU.subtract, ALU.mult)
                    i = v.tensor_tensor(OTS[tt % 2], TMP3, X1L[tt % 2], ALU.add)
                    k.inc(i, "wo_v", 1)

    def build(self, stop=None):
        self._stop = stop
        self._phn = 0
        self.qfree = {}
        nrk = DK // QW
        nv = DV // QW
        self._q_list = ([("wkq", i) for i in range(nrk)]
                        + [("wvq", i) for i in range(nv)]
                        + [("wrq", i) for i in range(nrk)]
                        + [("wkq", i) for i in range(nrk)]
                        + [("wvq", i) for i in range(nv)])
        Tm = self.T

        self._bind_views(LB)
        self._go(self.ph_load_consts, )
        self._go(self.ph_ln0, off=0)
        self._go(self.ph_z, )
        self._go(self.ph_branch_fm, 2, 0, self.Kb, DK, prefetch=[2, 3, 4, 5])
        self._w_prefetch = []
        self._go(self.ph_branch_fm, 1, None, self.THB, GLO, prefetch=[])
        self._go(self.ph_w_decay, extras=False)
        self._go(self.ph_branch_v, qb=4, prefetch=list(range(6, 12)))
        self._go(self.ph_sum, prefetch=[12, 13])
        self.dbg("sinit", self.S)

        self._bind_views(Tm)
        self._go(self.ph_ln0, off=LB)
        self.dbg("xnt", self.XNT)
        self._go(self.ph_z, )
        self.dbg("z0", self.Z0)
        self._go(self.ph_branch_fm, 0, 12, self.Rb, DK, prefetch=[14, 15, 16, 17])
        self._go(self.ph_branch_fm, 2, 16, self.Kb, DK, prefetch=[18, 19, 20, 21])
        self.dbg("rb", self.Rb)
        self.dbg("kb", self.Kb)
        self._w_prefetch = []
        self._go(self.ph_branch_fm, 1, None, self.THB, GLO, prefetch=[])
        self.dbg("thb", self.THB[0:64, 0:self.T])
        self._go(self.ph_w_decay, extras=True)
        self.dbg("qt", self.QT)
        self.dbg("kh", self.KH)
        self.dbg("dec", self.DEC)
        self.dbg("dtm", self.DTM)
        self._go(self.ph_branch_v, qb=20, prefetch=list(range(22, 28)))
        self.dbg("vtm", self.V[:, 0:self.NG * DV])
        self._go(self.ph_gnconst, )
        self._go(self.ph_p, )
        self.dbg("pm", self.PM)
        self._go(self.ph_chain, )
        self.dbg("ontm", self.ONTM)
        self._go(self.ph_trans, self.ONTM, self.ONFM, NCC)
        self._go(self.ph_lateg_xin, )
        self._go(self.ph_lateg_mm, )
        self.dbg("og", self.ONFM)
        self._go(self.ph_wo, )
        self.dbg("xff", self.XFF)
        self._go(self.ph_router, )
        self.dbg("cmb", self.CMB)
        self.dbg("sgc", self.SGC)
        self._go(self.ph_experts, )
        self.dbg("y0", self.Yt[0])
        self._go(self.ph_out, )
        self.finish()
        return self.nc

    def _go(self, fn, *a, **kw):
        if self._stop is not None and self._phn >= self._stop:
            return
        self._phn += 1
        fn(*a, **kw)

    def finish(self):
        k = self
        with self.nc.Block() as blk:
            @blk.sync
            def _(s):
                k.w(s, "o_dma", k.c["o_dma"])
        self.stack.close()


def _t(a):
    return np.ascontiguousarray(a)


def tiles_cm(w, out_w):
    return _t(w.reshape(NCC, 128, out_w).transpose(1, 0, 2).reshape(128, NCC * out_w))


def host_prepare(inputs, T, n_cores):
    I = {kk: np.asarray(vv) for kk, vv in inputs.items()}
    x = I["L0_attn_hidden_states"].reshape(-1, HID).astype(np.float32)
    W = {}
    W["wx1"] = tiles_cm(I["L0_attn_Wx1"].astype(np.float32), 160).astype(BF)
    wx2 = I["L0_attn_Wx2"].astype(np.float32)
    W["wx2a"] = _t(wx2[0:4].reshape(128, HID)).astype(BF)
    W["wx2b"] = _t(wx2[4]).astype(BF)
    W["ww1"] = tiles_cm(I["L0_attn_Ww1"].astype(np.float32), GLO).astype(BF)
    W["ww2"] = _t(I["L0_attn_Ww2"].astype(np.float32)).astype(BF)

    def quanta(w):
        out_w = w.shape[1]
        r = w.reshape(NCC, 128, out_w)
        return _t(np.stack(
            [r[:, :, i * QW:(i + 1) * QW].transpose(1, 0, 2).reshape(128, NCC * QW)
             for i in range(out_w // QW)]))

    W["wrq"] = quanta(I["L0_attn_Wr"].astype(np.float32))
    W["wkq"] = quanta(I["L0_attn_Wk"].astype(np.float32))
    W["wvq"] = quanta(I["L0_attn_Wv"].astype(np.float32))
    wg = I["L0_attn_Wg"].astype(np.float32).reshape(NCC, 128, 16, 128)
    W["wgs"] = _t(wg.transpose(2, 1, 0, 3).reshape(16, 128, NCC * 128))
    W["wos"] = _t(I["L0_attn_Wo"].astype(np.float32).reshape(16, 128, HID))
    W["bw"] = _t(I["L0_attn_bw"].astype(np.float32).reshape(8, 128).T)
    W["bonus"] = _t(I["L0_attn_bonus"].astype(np.float32).reshape(-1).reshape(8, 128).T)
    W["muxc"] = _t(I["L0_attn_mu_x"].astype(np.float32).reshape(NCC, 128).T)
    W["gnw_row"] = _t(np.tile(I["L0_attn_gn_w"].astype(np.float32), NH)[None, :])
    W["gnb_row"] = _t(np.tile(I["L0_attn_gn_b"].astype(np.float32), NH)[None, :])
    W["ident"] = np.eye(128, dtype=BF)
    p = np.arange(128)[:, None] % 32
    f = np.arange(128)[None, :] % 32
    W["maskp"] = _t((p < f).astype(BF))
    W["blkh4"] = _t((np.arange(128)[:, None] // 32 == np.arange(4)[None, :]).astype(BF))
    W["onesb"] = np.ones((128, 1), BF)
    W["gws"] = tiles_cm(I["L0_ffn_gate_w"].astype(np.float32).T, NE)
    W["sgws"] = tiles_cm(I["L0_ffn_shared_gate_w"].astype(np.float32).T, 1).reshape(128, NCC)

    def fc_tiles(w):
        F_ = w.shape[0]
        wt = w.T.reshape(NCC, 128, F_ // 128, 128)
        return _t(wt.transpose(2, 1, 0, 3).reshape(F_ // 128, 128, NCC * 128))

    fc1 = I["L0_ffn_expert_fc1"].astype(np.float32)
    fc2 = I["L0_ffn_expert_fc2"].astype(np.float32)
    pj = I["L0_ffn_expert_proj"].astype(np.float32)
    W["fc1s"] = _t(np.stack([fc_tiles(fc1[e]) for e in range(NE)]))
    W["fc2s"] = _t(np.stack([fc_tiles(fc2[e]) for e in range(NE)]))
    W["pjs"] = _t(np.stack([pj[e].T.reshape(NFC, 128, HID) for e in range(NE)]))
    W["sf1s"] = fc_tiles(I["L0_ffn_shared_fc1"].astype(np.float32))
    W["sf2s"] = fc_tiles(I["L0_ffn_shared_fc2"].astype(np.float32))
    W["spjs"] = _t(I["L0_ffn_shared_proj"].astype(np.float32).T.reshape(4 * NFC, 128, HID))

    maps = []
    for c in range(n_cores):
        m = dict(W)
        xs = np.zeros((1 + LB + T, HID), np.float32)
        t0 = c * T
        lo = max(0, t0 - LB - 1)
        span = x[lo:t0 + T]
        xs[1 + LB + T - span.shape[0]:] = span
        m["xsl"] = _t(xs)
        maps.append(m)
    return maps


def kernel(**inputs):
    T = 8192 // N_CORES
    maps = host_prepare(inputs, T, N_CORES)
    kb = KB(T, N_CORES, debug=False)
    nc = kb.build()
    res = run_bass_kernel_spmd(nc, maps, core_ids=list(range(N_CORES)))
    out = np.concatenate([r["out"] for r in res.results], axis=0)
    return out.reshape(1, 8192, HID)


# version 13 (final)
# speedup vs baseline: 1.0542x; 1.0542x over previous
from contextlib import ExitStack

import numpy as np
import ml_dtypes
import concourse.bass as bass
import concourse.mybir as mybir
from concourse.bass_utils import run_bass_kernel_spmd

F32 = mybir.dt.float32
BF16 = mybir.dt.bfloat16
AF = mybir.ActivationFunctionType
ALU = mybir.AluOpType
AX = mybir.AxisListType
BF = ml_dtypes.bfloat16

HID = 2048
NH = 32
DK = 1024
DV = 2048
RLO = 32
GLO = 64
EPS = 1e-5
NCC = HID // 128
CH = 32
N_CORES = 8
F_MOE = 1408
F_SH = 5632
NFC = F_MOE // 128
NE = 8
QW = 256
LB = 128

_DRAIN_OPS = {
    "tensor_copy", "tensor_tensor", "tensor_scalar", "tensor_single_scalar",
    "tensor_scalar_add", "tensor_scalar_sub", "tensor_scalar_mul",
    "tensor_scalar_max", "tensor_scalar_min", "tensor_add", "tensor_sub",
    "tensor_mul", "tensor_max", "tensor_reduce", "reduce_max", "reduce_sum",
    "scalar_tensor_tensor", "tensor_tensor_scan", "select", "reciprocal",
    "memset", "memzero", "activation", "copy", "mul", "add", "square", "sqrt",
    "sign", "bn_stats", "bn_aggr",
}


class Eng:

    def __init__(self, eng):
        self._e = eng
        self._dirty = False

    def __getattr__(self, name):
        attr = getattr(self._e, name)
        if name in _DRAIN_OPS:
            def wrapped(*a, **k):
                if self._dirty:
                    self._e.drain()
                self._dirty = True
                return attr(*a, **k)
            return wrapped
        return attr


class TEng:

    def __init__(self, eng):
        self._e = eng
        self._prev_key = None

    @staticmethod
    def _placement(out, lhsT, rhs, tp):
        bases = []
        for ap in (out, lhsT, rhs):
            try:
                bases.append(ap.base_partition())
            except Exception:
                bases.append(-1)
        if tp is None and bases == [0, 0, 0]:
            return None
        return (tp, tuple(bases))

    def matmul(self, out, lhsT, rhs, **kw):
        key = self._placement(out, lhsT, rhs, kw.get("tile_position"))
        if key != self._prev_key and (key is not None
                                      or self._prev_key is not None):
            self._e.drain()
        self._prev_key = key
        return self._e.matmul(out, lhsT, rhs, **kw)

    def transpose(self, *a, **kw):
        if self._prev_key is not None:
            self._e.drain()
            self._prev_key = None
        return self._e.transpose(*a, **kw)

    def __getattr__(self, name):
        return getattr(self._e, name)


class KB:

    def __init__(self, T, n_cores, debug=False):
        assert T % 128 == 0
        self.T = T
        self.NT = T // 128
        self.NCH = T // CH
        self.NG = T // 128
        self.n_cores = n_cores
        self.debug = debug
        self.nc = bass.Bass("TRN2", target_bir_lowering=False)
        self.stack = ExitStack()
        self.D = {}
        self.dbg_outs = []
        self.sems = {}
        self.c = {}
        self._declare_dram()
        self._alloc_sbuf()
        self._alloc_sems()

    def inc(self, instr, name, n=1):
        instr.then_inc(self.sems[name], n)
        self.c[name] += n
        return self.c[name]

    def w(self, eng, name, val):
        eng.wait_ge(self.sems[name], val)

    def din(self, name, shape, dt=F32):
        self.D[name] = self.nc.dram_tensor(
            name, list(shape), dt, kind="ExternalInput").ap()

    def dint(self, name, shape, dt=F32):
        self.D[name] = self.nc.dram_tensor(name, list(shape), dt).ap()

    def sb(self, name, shape, dt):
        return self.stack.enter_context(
            self.nc.sbuf_tensor(name, list(shape), dt)).ap()

    def ps(self, stk, name, shape, dt=F32):
        self._uid = getattr(self, "_uid", 0) + 1
        return stk.enter_context(
            self.nc.psum_tensor(f"{name}_{self._uid}", list(shape), dt)).ap()

    def dbg(self, name, ap):
        if not self.debug:
            return
        d = self.nc.dram_tensor("dbg_" + name, list(ap.shape), ap.dtype,
                                kind="ExternalOutput").ap()
        self.dbg_outs.append("dbg_" + name)
        k = self
        with self.nc.Block() as blk:
            @blk.sync
            def _(s):
                n = k.inc(s.dma_start(out=d, in_=ap), "o_dma", 16)
                k.w(s, "o_dma", n)

    def mm(self, t, out, lhsT, rhs, start=True, stop=True, tp=None):
        N = rhs.shape[-1]
        last = None
        o = 0
        while o < N:
            w_ = min(512, N - o)
            last = t.matmul(out[:, o:o + w_], lhsT, rhs[:, o:o + w_],
                            start=start, stop=stop, tile_position=tp)
            o += w_
        return last

    def _declare_dram(self):
        T = self.T
        din, dint = self.din, self.dint
        din("xsl", [1 + LB + T, HID])
        din("wx1", [128, NCC * 160], BF16)
        din("wx2a", [128, HID], BF16)
        din("wx2b", [32, HID], BF16)
        din("ww1", [128, NCC * GLO], BF16)
        din("ww2", [64, DK], BF16)
        din("wrq", [DK // QW, 128, NCC * QW])
        din("wkq", [DK // QW, 128, NCC * QW])
        din("wvq", [DV // QW, 128, NCC * QW])
        din("wgs", [16, 128, NCC * 128])
        din("wos", [16, 128, HID])
        din("bw", [128, 8], F32)
        din("bonus", [128, 8], F32)
        din("muxc", [128, NCC], F32)
        din("gnw_row", [1, DV], F32)
        din("gnb_row", [1, DV], F32)
        din("ident", [128, 128], BF16)
        din("maskp", [128, 128], BF16)
        din("blkh4", [128, 4], BF16)
        din("onesb", [128, 1], BF16)
        din("gws", [128, NCC * NE])
        din("sgws", [128, NCC])
        din("fc1s", [NE, NFC, 128, NCC * 128])
        din("fc2s", [NE, NFC, 128, NCC * 128])
        din("pjs", [NE, NFC, 128, HID])
        din("sf1s", [4 * NFC, 128, NCC * 128])
        din("sf2s", [4 * NFC, 128, NCC * 128])
        din("spjs", [4 * NFC, 128, HID])
        self.out = self.nc.dram_tensor("out", [T, HID], F32,
                                       kind="ExternalOutput").ap()
        dint("x1d", [T, HID])

    def _alloc_sbuf(self):
        T, NT, NCH, NG = self.T, self.NT, self.NCH, self.NG
        sb = self.sb
        AW = max(16 * T, 16384)
        self.AW = AW
        self.arA = sb("arA", [128, AW], BF16)
        self.arB = sb("arB", [128, AW], BF16)
        self.arC = sb("arC", [128, AW], BF16)
        self.XNT_t = sb("xnt", [128, max(NCC * (T + 1), 16384)], BF16)
        self.V = sb("vtm", [128, AW], BF16)
        self.WB = sb("wb0", [128, 2 * NCC * QW], BF16)
        arAf = self.arA.bitcast(F32)
        self.arAf = arAf
        self.arBf = self.arB.bitcast(F32)
        self.arCf = self.arC.bitcast(F32)
        WBf = self.WB.bitcast(F32)

        self._bind_views(T)

        self.Z0 = self.V[:, 0:T]
        self.Z1 = sb("z1", [32, T], BF16)
        self.WX1 = self.V[:, 4096:4096 + NCC * 160]
        self.WX2A = sb("wx2as", [128, HID], BF16)
        self.WX2B = sb("wx2bs", [32, HID], BF16)
        self.WW1 = sb("ww1s", [128, NCC * GLO], BF16)
        self.WW2 = sb("ww2s", [64, DK], BF16)
        self.MUXC = sb("muxcs", [128, NCC], F32)
        self.BW = sb("bws", [128, 8], F32)
        self.BONUS = sb("bonuss", [128, 8], F32)
        self.IDENT = sb("idents", [128, 128], BF16)
        self.MASKP = sb("maskps", [128, 128], BF16)
        self.BLKH4 = sb("blkh4s", [128, 4], BF16)
        self.ONESB = sb("onesbs", [128, 1], BF16)
        self.S = sb("ssb", [128, 512], F32)
        self.SB16T = sb("sb16", [128, 1024], BF16)
        self.SB16B = [self.SB16T[:, 0:512], self.SB16T[:, 512:1024]]
        self.MMB = sb("mmbs", [128, 8 * (NCH + 1)], F32)
        self.DEC = sb("decs", [128, 8 * NCH], F32)
        self.DTS = self.V[0:4, 1024:1024 + T]
        self.DTM = sb("dtms", [128, NG * 32], BF16)
        self.HALOT = sb("halots", [128, NCC], BF16)
        self.ST1 = sb("st1", [128, 16], F32)
        self.GSIL = sb("gsils", [128, max(T, 1024)], BF16)
        self.SILS = sb("sils", [128, max(T, 1024)], BF16)
        self.DELTA = sb("deltas", [128, 1024], BF16)
        self.DELTAG = sb("deltag", [128, 2048], BF16)
        self.KHTG = self.DELTA[:, 0:1024]
        self.KHTGB = [self.DELTA[:, 0:1024], self.SILS[:, 0:1024]]
        self.GWSB = sb("gwsb", [128, NCC * NE], BF16)
        self.SGWB = sb("sgwb", [128, NCC], BF16)
        self.THB = self.GSIL
        self.RKB = self.SILS
        gsf = self.GSIL.bitcast(F32)
        self.GNS = gsf[:, 0:128]
        self.CMB = gsf[:, 128:128 + 8 * NT]
        self.SGC = gsf[:, 224:224 + NT]
        self.PR8 = gsf[:, 240:252]
        self.STW = gsf[:, 252:260]

    def _bind_views(self, T):
        self.T = T
        self.NT = max(1, T // 128)
        self.NCH = T // CH
        self.NG = max(1, T // 128)
        NT, NCH, NG = self.NT, self.NCH, self.NG
        arAf = self.arAf
        WBf = self.WB.bitcast(F32)
        self.XNT = self.XNT_t[:, 0:NCC * (T + 1)]

        self.XT = [arAf[:, 0:HID], arAf[:, HID:2 * HID]]
        self.SCF = arAf[:, 2 * HID:3 * HID]
        self.X1T = self.V.bitcast(F32)[:, HID:2 * HID]
        self.SCR = [self.arA[:, 6 * HID:7 * HID], self.arA[:, 7 * HID:8 * HID]]
        self.MUX = self.arA[:, 0:NCC * 512]
        self.WEXPC = arAf[:, 4096:4096 + T]
        self.CGL = arAf[:, 4096 + T:4096 + 2 * T]
        self.CNL = arAf[:, 4096 + 2 * T:4096 + 3 * T]
        self.E1S = self.arA[:, 8192 + 6 * T:8192 + 7 * T]
        self.E2S = self.arA[:, 8192 + 7 * T:8192 + 8 * T]
        self.MUXF = self.arB
        self.Rb = self.arB[:, 0:8 * T]
        self.Kb = self.arB[:, 8 * T:16 * T]
        self.QT = self.arC[:, 0:8 * T]
        self.KH = self.arC[:, 8 * T:16 * T]
        self.PM = self.arB[:, 0:8 * T]
        self.Og = [self.arA[:, gr * DV:(gr + 1) * DV] for gr in range(NG)]
        self.ONTM = self.arA[:, 0:16 * T]
        self.ONFM = self.arB[:, 0:16 * T]
        self.XIN = self.V[:, 0:16 * T]
        self.XFF = self.arC[:, 0:16 * T]
        self.Ht = self.V[:, 0:NFC * T]
        self.SQB = self.V[:, 0:DV]
        self.SQC = self.arB[:, 8 * T:8 * T + DV]
        self.Yt = [self.arA[:, tt * HID:(tt + 1) * HID] for tt in range(NT)]

        self.GNW = WBf[:, 0:DV]
        self.GNB = WBf[:, DV:2 * DV]
        self.WOSL = [self.WB[:, 0:2048], self.WB[:, 2048:4096]]
        self.WGSL = [self.WB[:, 4096:6144], self.WB[:, 6144:8192]]
        self.FC1SL = [self.WB[:, 0:2048], self.WB[:, 2048:4096],
                      self.arB[:, 8192:10240], self.arB[:, 10240:12288]]
        self.FC2SL = [self.WB[:, 4096:6144], self.WB[:, 6144:8192],
                      self.arB[:, 12288:14336], self.arB[:, 14336:16384]]


    def _alloc_sems(self):
        names = [
            "cst", "cstg", "wq0", "wq1", "wq2", "wq3", "wfree", "xld0", "xld1", "v_a", "s_a", "tr_mm", "tr_ev",
            "mu_mm", "xin_v", "xin_g", "fe_mm", "fe_ev", "dec_v", "dec_s",
            "e_a",
            "sum_mm", "sum_up", "ccs", "fld_d", "p_mm", "pm_ev",
            "c_mm", "c_up", "c_ev", "c_oi", "gn_v", "lg_mm", "lg_s", "lg_v",
            "wo_mm", "wo_v", "r_mm", "h_mm", "h_s", "h_v",
            "y_mm", "y_v", "o_dma", "sp_d", "gq0", "gq1", "woq0", "woq1", "wor",
            "fq0", "fq1", "fq2", "fq3", "f2q0", "f2q1", "f2q2", "f2q3",
            "pq0", "pq1", "pq2", "pq3", "x1s", "fld0", "fld1",
            "rt_mm", "rt_v",
        ]
        for n in names:
            self.sems[n] = self.stack.enter_context(self.nc.semaphore(n))
            self.c[n] = 0

    def _q_slot(self, gi):
        W = NCC * QW
        return self.WB[:, (gi % 2) * W:(gi % 2 + 1) * W]

    def _q_load(self, g, gi):
        name, idx = self._q_list[gi]
        if gi >= 2:
            sem, val = self.qfree[gi - 2]
            self.w(g, sem, val)
        self.inc(g.dma_start(out=self._q_slot(gi), in_=self.D[name][idx]),
                 f"wq{gi % 2}", 16)

    def _q_wait(self, eng, gi):
        self.w(eng, f"wq{gi % 2}", 16 * (gi // 2 + 1))

    def _wo_q(self, dc):
        if dc < 8:
            return self.XNT_t[:, dc * 2048:(dc + 1) * 2048]
        if dc < 10:
            return self.WB[:, (dc - 8) * 2048:(dc - 7) * 2048]
        if dc < 12:
            return self.V[:, (dc - 10) * 2048:(dc - 9) * 2048]
        if dc < 14:
            return self.V[:, 12288 + (dc - 12) * 2048:
                          12288 + (dc - 11) * 2048]
        return self.WB[:, 4096 + (dc - 14) * 2048:4096 + (dc - 13) * 2048]

    def ph_load_consts(self):
        k = self
        with self.nc.Block() as blk:
            @blk.sync
            def _(s):
                for n, dst in [
                        ("muxc", k.MUXC), ("bw", k.BW), ("bonus", k.BONUS),
                        ("ident", k.IDENT), ("maskp", k.MASKP),
                        ("blkh4", k.BLKH4), ("onesb", k.ONESB),
                        ("wx1", k.WX1),
                        ("wx2a", k.WX2A), ("wx2b", k.WX2B), ("ww1", k.WW1),
                        ("ww2", k.WW2)]:
                    k.inc(s.dma_start(out=dst, in_=k.D[n]), "cst", 16)
                k.w(s, "cst", k.c["cst"])

            @blk.gpsimd
            def _(g):
                k.inc(g.dma_start(out=k.GWSB, in_=k.D["gws"]), "cstg", 16)
                k.inc(g.dma_start(out=k.SGWB, in_=k.D["sgws"]), "cstg", 16)
                for gi in range(2):
                    self._q_load(g, gi)
                k.w(g, "cstg", k.c["cstg"])

    def ph_ln0(self, off=LB):
        k = self
        T, NT = self.T, self.NT
        with self.nc.Block() as blk, ExitStack() as pstk:
            tps = [k.ps(pstk, "tps0", [128, 128], BF16),
                   k.ps(pstk, "tps1", [128, 128], BF16)]
            hps = k.ps(pstk, "hps", [128, NCC], F32)
            s0, v0 = k.c["s_a"], k.c["v_a"]
            t0_, e0 = k.c["tr_mm"], k.c["tr_ev"]
            x0 = [k.c["xld0"], k.c["xld1"]]

            @blk.sync
            def _(s):
                for i in range(NT + 1):
                    if i >= 2:
                        k.w(s, "v_a", v0 + 2 * (i - 1))
                        k.w(s, "s_a", s0 + 2 * i - 3)
                    src = (k.D["xsl"][off + 1 + i * 128:off + 1 + (i + 1) * 128, :]
                           if i < NT else k.D["xsl"][off:off + 1, :])
                    dst = k.XT[i % 2] if i < NT else k.XT[i % 2][0:1, :]
                    k.inc(s.dma_start(out=dst, in_=src), f"xld{i % 2}", 16)

            @blk.vector
            def _(v):
                v = Eng(v)
                for i in range(NT + 1):
                    rows = 128 if i < NT else 1
                    xt = k.XT[i % 2][0:rows, :]
                    k.w(v, f"xld{i % 2}", x0[i % 2] + 16 * (i // 2 + 1))
                    v.tensor_reduce(k.ST1[0:rows, 0:1], xt, AX.X, ALU.add)
                    v.tensor_scalar(k.ST1[0:rows, 1:2], k.ST1[0:rows, 0:1],
                                    1.0 / HID, None, ALU.mult)
                    k.w(v, "s_a", s0 + 2 * i + 1)
                    v.tensor_scalar(k.ST1[0:rows, 2:3], k.ST1[0:rows, 4:5],
                                    1.0 / HID, EPS, ALU.mult, ALU.add)
                    v.tensor_tensor(k.ST1[0:rows, 3:4], k.ST1[0:rows, 1:2],
                                    k.ST1[0:rows, 1:2], ALU.mult)
                    v.tensor_tensor(k.ST1[0:rows, 2:3], k.ST1[0:rows, 2:3],
                                    k.ST1[0:rows, 3:4], ALU.subtract)
                    i1 = v.reciprocal(k.ST1[0:rows, 6:7], k.ST1[0:rows, 2:3])
                    k.inc(i1, "v_a", 1)
                    k.w(v, "s_a", s0 + 2 * i + 2)
                    if i >= 2:
                        k.w(v, "tr_mm", t0_ + (i - 1) * NCC)
                    i2 = v.tensor_scalar(k.SCR[i % 2][0:rows, :], xt,
                                         k.ST1[0:rows, 1:2], k.ST1[0:rows, 5:6],
                                         ALU.subtract, ALU.mult)
                    k.inc(i2, "v_a", 1)
                    if 1 <= i <= NT:
                        for cc in range(NCC):
                            n = (i - 1) * NCC + cc
                            k.w(v, "tr_mm", t0_ + n + 1)
                            b = cc * (T + 1)
                            ev = v.tensor_copy(
                                k.XNT[:, b + 1 + (i - 1) * 128:b + 1 + i * 128],
                                tps[n % 2])
                            k.inc(ev, "tr_ev", 1)
                k.w(v, "tr_mm", t0_ + NT * NCC + 1)
                v.tensor_copy(k.HALOT, hps)
                ev = v.tensor_copy(
                    k.XNT.rearrange("p (c t) -> p c t", c=NCC)[:, :, 0:1],
                    k.HALOT.unsqueeze(2))
                k.inc(ev, "tr_ev", 1)

            @blk.scalar
            def _(sc):
                sc = Eng(sc)
                for i in range(NT + 1):
                    rows = 128 if i < NT else 1
                    xt = k.XT[i % 2][0:rows, :]
                    k.w(sc, f"xld{i % 2}", x0[i % 2] + 16 * (i // 2 + 1))
                    a = sc.activation(k.SCF[0:rows, :], xt, AF.Square,
                                      accum_out=k.ST1[0:rows, 4:5])
                    k.inc(a, "s_a", 1)
                    k.w(sc, "v_a", v0 + 2 * i + 1)
                    a = sc.activation(k.ST1[0:rows, 5:6], k.ST1[0:rows, 6:7],
                                      AF.Sqrt)
                    k.inc(a, "s_a", 1)

            @blk.tensor
            def _(t):
                t = TEng(t)
                for i in range(NT):
                    k.w(t, "v_a", v0 + 2 * i + 2)
                    for cc in range(NCC):
                        n = i * NCC + cc
                        if n >= 2:
                            k.w(t, "tr_ev", e0 + n - 1)
                        mm = t.transpose(tps[n % 2],
                                         k.SCR[i % 2][:, cc * 128:(cc + 1) * 128],
                                         k.IDENT)
                        k.inc(mm, "tr_mm", 1)
                k.w(t, "v_a", v0 + 2 * NT + 2)
                for cc in range(NCC):
                    mm = t.matmul(hps[:, cc:cc + 1],
                                  k.SCR[NT % 2][0:1, cc * 128:(cc + 1) * 128],
                                  k.ONESB[0:1, 0:1], start=True, stop=True)
                k.inc(mm, "tr_mm", 1)

    def ph_z(self):
        k = self
        T = self.T
        with self.nc.Block() as blk, ExitStack() as pstk:
            zp0 = k.ps(pstk, "zp0", [128, T], F32)
            zp1 = k.ps(pstk, "zp1", [32, T], F32)
            xin0, mu0 = k.c["xin_v"], k.c["mu_mm"]

            @blk.vector
            def _(v):
                v = Eng(v)
                for cc in range(NCC):
                    b = cc * (T + 1)
                    if cc >= 1:
                        k.w(v, "mu_mm", mu0 + cc)
                    dl = k.DELTA[:, 0:T]
                    v.tensor_tensor(dl, k.XNT[:, b:b + T],
                                    k.XNT[:, b + 1:b + T + 1], ALU.subtract)
                    i2 = v.scalar_tensor_tensor(dl, dl,
                                                k.MUXC[:, cc:cc + 1],
                                                k.XNT[:, b + 1:b + T + 1],
                                                ALU.mult, ALU.add)
                    k.inc(i2, "xin_v", 1)

            @blk.tensor
            def _(t):
                t = TEng(t)
                for cc in range(NCC):
                    k.w(t, "xin_v", xin0 + cc + 1)
                    st, sp = cc == 0, cc == NCC - 1
                    k.mm(t, zp0, k.WX1[:, cc * 160:cc * 160 + 128],
                         k.DELTA[:, 0:T], start=st, stop=sp)
                    last = k.mm(t, zp1[0:32, :],
                                k.WX1[:, cc * 160 + 128:cc * 160 + 160],
                                k.DELTA[:, 0:T], start=st, stop=sp)
                    k.inc(last, "mu_mm", 1)

            @blk.scalar
            def _(sc):
                sc = Eng(sc)
                k.w(sc, "mu_mm", k.c["mu_mm"])
                sc.activation(k.Z0[:, 0:T], zp0, AF.Tanh)
                sc.activation(k.Z1[0:32, 0:T], zp1[0:32, :], AF.Tanh)

    def _xin_sub(self, ge, cc, tw, xv0):
        k = self
        b = cc * (self.T + 1)
        dl = k.DELTAG[:, (cc % 2) * 1024:(cc % 2) * 1024 + tw]
        if cc >= 2:
            k.w(ge, "xin_v", xv0 + cc - 1)
        i = ge.tensor_tensor(dl, k.XNT[:, b:b + tw],
                             k.XNT[:, b + 1:b + tw + 1], ALU.subtract)
        k.inc(i, "xin_g", 1)

    def _xin_mu(self, v, cc, tw, mup, dst, dst_col):
        k = self
        b = cc * (self.T + 1)
        dl = k.DELTAG[:, (cc % 2) * 1024:(cc % 2) * 1024 + tw]
        v.tensor_tensor(dl, dl, mup, ALU.mult)
        return v.tensor_tensor(dst[:, dst_col:dst_col + tw], dl,
                               k.XNT[:, b + 1:b + tw + 1], ALU.add)

    def _xin_dve(self, v, cc, t0, tw, mup, dst, dst_col, tmp=None):
        k = self
        dl = (tmp if tmp is not None else k.DELTA)[:, 0:tw]
        b = cc * (self.T + 1) + t0
        v.tensor_tensor(dl, k.XNT[:, b:b + tw],
                        k.XNT[:, b + 1:b + tw + 1], ALU.subtract)
        v.tensor_tensor(dl, dl, mup, ALU.mult)
        return v.tensor_tensor(dst[:, dst_col:dst_col + tw], dl,
                               k.XNT[:, b + 1:b + tw + 1], ALU.add)

    def ph_branch_fm(self, br, qbase, dst, out_w, prefetch):
        k = self
        T, NT = self.T, self.NT
        n_os = max(1, out_w // 128)
        qd = max(1, QW // 128)
        rows = 128 if out_w >= 128 else out_w
        MUXT = self.arA[:, 0:NCC * T]
        with self.nc.Block() as blk, ExitStack() as pstk:
            mup = [k.ps(pstk, "mup0", [128, T], F32),
                   k.ps(pstk, "mup1", [128, T], F32)]
            pc = [k.ps(pstk, "pc0", [rows, T], F32),
                  k.ps(pstk, "pc1", [rows, T], F32)]
            mu0, xin0 = k.c["mu_mm"], k.c["xin_v"]
            xg0 = k.c["xin_g"]
            fe0, ev0 = k.c["fe_mm"], k.c["fe_ev"]
            tp = (32 * br, 0)

            @blk.tensor
            def _(t):
                t = TEng(t)
                for cc in range(NCC):
                    if cc >= 2:
                        k.w(t, "xin_v", xin0 + cc - 1)
                    mm = k.mm(t, mup[cc % 2],
                              k.WX2A[32 * br:32 * br + 32,
                                     cc * 128:(cc + 1) * 128],
                              k.Z0[32 * br:32 * br + 32, 0:T],
                              start=True, stop=True, tp=tp)
                    k.inc(mm, "mu_mm", 1)
                k.w(t, "xin_v", xin0 + NCC)
                for os in range(n_os):
                    if os >= 2:
                        k.w(t, "fe_ev", ev0 + os - 1)
                    if qbase is not None:
                        self._q_wait(t, qbase + os // qd)
                    for cc in range(NCC):
                        if qbase is not None:
                            slot = self._q_slot(qbase + os // qd)
                            qo = (os % qd) * 128
                            lhsT = slot[:, cc * QW + qo:cc * QW + qo + 128]
                        else:
                            lhsT = k.WW1[:, cc * GLO:(cc + 1) * GLO]
                        mm = k.mm(t, pc[os % 2][0:rows, :], lhsT,
                                  MUXT[:, cc * T:(cc + 1) * T],
                                  start=(cc == 0), stop=(cc == NCC - 1))
                    k.inc(mm, "fe_mm", 1)
                    if qbase is not None and (os % qd) == qd - 1:
                        k.qfree[qbase + os // qd] = ("fe_mm", k.c["fe_mm"])

            @blk.vector
            def _(v):
                v = Eng(v)
                for cc in range(NCC):
                    k.w(v, "mu_mm", mu0 + cc + 1)
                    k.w(v, "xin_g", xg0 + cc + 1)
                    i2 = self._xin_mu(v, cc, T, mup[cc % 2], MUXT, cc * T)
                    k.inc(i2, "xin_v", 1)
                for os in range(n_os):
                    k.w(v, "fe_mm", fe0 + os + 1)
                    ev = v.tensor_copy(dst[0:rows, os * T:(os + 1) * T],
                                       pc[os % 2][0:rows, :])
                    k.inc(ev, "fe_ev", 1)

            @blk.gpsimd
            def _(g):
                ge = Eng(g)
                for cc in range(NCC):
                    self._xin_sub(ge, cc, T, xin0)
                for gi in prefetch:
                    self._q_load(g, gi)

    def ph_w_decay(self, extras=True):
        k = self
        T, NT, NCH, NG = self.T, self.NT, self.NCH, self.NG
        dvk = 3 if extras else 2
        smk = 2 if extras else 1
        ssk = 3
        with self.nc.Block() as blk, ExitStack() as pstk:
            pb = [k.ps(pstk, "pbw0", [128, T], F32),
                  k.ps(pstk, "pbw1", [128, T], F32)]
            if extras:
                dps = k.ps(pstk, "dps", [4, T], F32)
                tp4 = [k.ps(pstk, "tp40", [128, 4], BF16),
                       k.ps(pstk, "tp41", [128, 4], BF16)]
            s0, v0, e0 = k.c["dec_s"], k.c["dec_v"], k.c["e_a"]
            m0 = k.c["sum_mm"]
            t0_, te0 = k.c["tr_mm"], k.c["tr_ev"]
            NB = NCH + 1

            @blk.tensor
            def _(t):
                t = TEng(t)
                for ks in range(8):
                    k.w(t, "dec_s", s0 + 1 + max(0, ssk * (ks - 2) + 1) if ks >= 2
                        else s0 + 1)
                    mm = k.mm(t, pb[ks % 2], k.WW2[0:64, ks * 128:(ks + 1) * 128],
                              k.THB[0:64, 0:T], start=True, stop=True)
                    k.inc(mm, "sum_mm", 1)
                    if not extras:
                        continue
                    k.w(t, "dec_v", v0 + dvk * ks + 3)
                    if ks >= 1:
                        k.w(t, "e_a", e0 + ks)
                    mm = k.mm(t, dps[0:4, :], k.BLKH4, k.RKB[:, 0:T],
                              start=True, stop=True)
                    k.inc(mm, "sum_mm", 1)
                    k.w(t, "e_a", e0 + ks + 1)
                    for gr in range(NG):
                        n = NG * ks + gr
                        if n >= 2:
                            k.w(t, "tr_ev", te0 + n - 1)
                        mm = t.transpose(tp4[n % 2],
                                         k.DTS[0:4, gr * 128:(gr + 1) * 128],
                                         k.IDENT[0:4, 0:4])
                        k.inc(mm, "tr_mm", 1)

            @blk.scalar
            def _(sc):
                sc = Eng(sc)
                a = sc.activation(k.THB[0:64, 0:T], k.THB[0:64, 0:T], AF.Tanh)
                k.inc(a, "dec_s", 1)
                for ks in range(8):
                    k.w(sc, "sum_mm", m0 + smk * ks + 1)
                    if ks >= 1:
                        k.w(sc, "dec_v", v0 + dvk * (ks - 1) + 1)
                    a = sc.activation(k.WEXPC, pb[ks % 2], AF.Exp,
                                      bias=k.BW[:, ks:ks + 1])
                    k.inc(a, "dec_s", 1)
                    k.w(sc, "dec_v", v0 + dvk * ks + 1)
                    sc.activation(k.E2S, k.CNL, AF.Exp)
                    a = sc.activation(k.E1S, k.CGL, AF.Exp, scale=-1.0)
                    k.inc(a, "dec_s", 1)
                    a = sc.activation(k.DEC[:, ks * NCH:(ks + 1) * NCH],
                                      k.DEC[:, ks * NCH:(ks + 1) * NCH],
                                      AF.Exp, scale=-1.0)
                    k.inc(a, "dec_s", 1)

            @blk.vector
            def _(v):
                v = Eng(v)
                for ks in range(8):
                    NBo = ks * NB
                    k.w(v, "dec_s", s0 + 1 + ssk * ks + 1)
                    v.tensor_tensor_scan(k.CGL, k.WEXPC, k.WEXPC, 0.0,
                                         ALU.add, ALU.bypass)
                    v.memset(k.MMB[:, NBo:NBo + 1], 0.0)
                    v.tensor_copy(k.MMB[:, NBo + 1:NBo + NB],
                                  k.CGL[:, CH - 1::CH])
                    v.tensor_tensor(k.DEC[:, ks * NCH:(ks + 1) * NCH],
                                    k.MMB[:, NBo + 1:NBo + NB],
                                    k.MMB[:, NBo:NBo + NB - 1], ALU.subtract)
                    v.tensor_tensor(
                        k.CNL.rearrange("p (a b) -> p a b", a=NCH),
                        k.CGL.rearrange("p (a b) -> p a b", a=NCH),
                        k.MMB[:, NBo:NBo + NCH].unsqueeze(2).broadcast_to(
                            (128, NCH, CH)), ALU.subtract)
                    i1 = v.tensor_tensor(k.CGL, k.CNL, k.WEXPC, ALU.subtract)
                    k.inc(i1, "dec_v", 1)
                    k.w(v, "dec_s", s0 + 1 + ssk * ks + 2)
                    if extras:
                        v.tensor_tensor(k.QT[:, ks * T:(ks + 1) * T],
                                        k.Rb[:, ks * T:(ks + 1) * T], k.E1S,
                                        ALU.mult)
                    i2 = v.tensor_tensor(k.KH[:, ks * T:(ks + 1) * T],
                                         k.Kb[:, ks * T:(ks + 1) * T], k.E2S,
                                         ALU.mult)
                    k.inc(i2, "dec_v", 1)
                    if not extras:
                        continue
                    i3 = v.scalar_tensor_tensor(k.RKB[:, 0:T],
                                                k.Rb[:, ks * T:(ks + 1) * T],
                                                k.BONUS[:, ks:ks + 1],
                                                k.Kb[:, ks * T:(ks + 1) * T],
                                                ALU.mult, ALU.mult)
                    k.inc(i3, "dec_v", 1)
                    k.w(v, "sum_mm", m0 + smk * ks + 2)
                    if ks >= 1:
                        k.w(v, "tr_mm", t0_ + ks * NG)
                    ev = v.tensor_copy(k.DTS[0:4, 0:T], dps[0:4, :])
                    k.inc(ev, "e_a", 1)
                    for gr in range(NG):
                        n = NG * ks + gr
                        k.w(v, "tr_mm", t0_ + n + 1)
                        ev = v.tensor_copy(
                            k.DTM[:, gr * 32 + ks * 4:gr * 32 + (ks + 1) * 4],
                            tp4[n % 2])
                        k.inc(ev, "tr_ev", 1)

            @blk.gpsimd
            def _(g):
                for gi in self._w_prefetch:
                    self._q_load(g, gi)

    def ph_branch_v(self, qb, prefetch):
        k = self
        T, NT = self.T, self.NT
        nq = DV // QW
        with self.nc.Block() as blk, ExitStack() as pstk:
            mup = [k.ps(pstk, "mupv0", [128, T], F32),
                   k.ps(pstk, "mupv1", [128, T], F32)]
            pcv = [k.ps(pstk, "pcv0", [128, QW], F32),
                   k.ps(pstk, "pcv1", [128, QW], F32),
                   k.ps(pstk, "pcv2", [128, QW], F32),
                   k.ps(pstk, "pcv3", [128, QW], F32)]
            mu0, xin0 = k.c["mu_mm"], k.c["xin_v"]
            xg0 = k.c["xin_g"]
            fe0, ev0 = k.c["fe_mm"], k.c["fe_ev"]

            @blk.tensor
            def _(t):
                t = TEng(t)
                for cc in range(NCC):
                    if cc >= 2:
                        k.w(t, "xin_v", xin0 + cc - 1)
                    mm = k.mm(t, mup[cc % 2],
                              k.WX2A[96:128, cc * 128:(cc + 1) * 128],
                              k.Z0[96:128, 0:T], start=True, stop=True,
                              tp=(96, 0))
                    k.inc(mm, "mu_mm", 1)
                k.w(t, "xin_v", xin0 + NCC)
                for qp in range(nq // 2):
                    self._q_wait(t, qb + 2 * qp)
                    self._q_wait(t, qb + 2 * qp + 1)
                    for tt in range(NT):
                        n = qp * NT + tt
                        if n >= 2:
                            k.w(t, "fe_ev", ev0 + 2 * (n - 1))
                        for cc in range(NCC):
                            lhsT = k.MUXF[:, cc * T + tt * 128:cc * T + (tt + 1) * 128]
                            for j in range(2):
                                slot = self._q_slot(qb + 2 * qp + j)
                                mm = k.mm(t, pcv[2 * (n % 2) + j],
                                          lhsT, slot[:, cc * QW:(cc + 1) * QW],
                                          start=(cc == 0), stop=(cc == NCC - 1))
                        k.inc(mm, "fe_mm", 1)
                    k.qfree[qb + 2 * qp] = ("fe_mm", k.c["fe_mm"])
                    k.qfree[qb + 2 * qp + 1] = ("fe_mm", k.c["fe_mm"])

            @blk.vector
            def _(v):
                v = Eng(v)
                for cc in range(NCC):
                    k.w(v, "mu_mm", mu0 + cc + 1)
                    k.w(v, "xin_g", xg0 + cc + 1)
                    i2 = self._xin_mu(v, cc, T, mup[cc % 2], k.MUXF, cc * T)
                    k.inc(i2, "xin_v", 1)
                for qp in range(nq // 2):
                    for tt in range(NT):
                        n = qp * NT + tt
                        k.w(v, "fe_mm", fe0 + n + 1)
                        for j in range(2):
                            vc = (2 * qp + j) * QW
                            ev = v.tensor_copy(
                                k.V[:, tt * DV + vc:tt * DV + vc + QW],
                                pcv[2 * (n % 2) + j])
                            k.inc(ev, "fe_ev", 1)

            @blk.gpsimd
            def _(g):
                ge = Eng(g)
                for cc in range(NCC):
                    self._xin_sub(ge, cc, T, xin0)
                for gi in prefetch:
                    self._q_load(g, gi)


    def _contrib_mms(self, t, ch, CP, buf=0):
        k = self
        gr, j = ch // 4, ch % 4
        last = None
        for sub in range(4):
            for hs in range(8):
                h = 4 * hs + sub
                lhsT = k.KHTGB[buf][32 * j:32 * j + 32,
                                    hs * 128 + sub * 32:
                                    hs * 128 + sub * 32 + 32]
                rhs = k.V[32 * j:32 * j + 32,
                          gr * DV + 64 * h:gr * DV + 64 * h + 64]
                last = t.matmul(CP[32 * sub:32 * sub + 32,
                                   64 * hs:64 * hs + 64],
                                lhsT, rhs, start=True, stop=True,
                                tile_position=(32 * j, 32 * sub))
        return last

    def _khtg_mms(self, t, k_, gr, tq, base_n, te0, t0_):
        for ks in range(8):
            n = base_n + ks
            if n >= 2:
                k_.w(t, "tr_ev", te0 + n - 1)
            mm = t.transpose(tq[n % 2],
                             k_.KH[:, ks * k_.T + gr * 128:
                                   ks * k_.T + (gr + 1) * 128], k_.IDENT)
            k_.inc(mm, "tr_mm", 1)

    def _khtg_evs(self, v, k_, base_n, tq, t0_, buf=0):
        for ks in range(8):
            n = base_n + ks
            k_.w(v, "tr_mm", t0_ + n + 1)
            ev = v.tensor_copy(k_.KHTGB[buf][:, ks * 128:(ks + 1) * 128],
                               tq[n % 2])
            k_.inc(ev, "tr_ev", 1)

    def _s_update(self, v, ch, CP):
        k = self
        NCH = self.NCH
        s3 = k.S.rearrange("p (g v) -> p g v", g=8)
        v.tensor_tensor(s3, s3, CP.rearrange("p (g v) -> p g v", g=8), ALU.add)
        decb = k.DEC[:, ch:8 * NCH:NCH].unsqueeze(2).broadcast_to((128, 8, 64))
        return v.tensor_tensor(s3, s3, decb, ALU.mult)

    def ph_sum(self, prefetch=()):
        k = self
        NCH, NG = self.NCH, self.NG
        with self.nc.Block() as blk, ExitStack() as pstk:
            CP = k.ps(pstk, "cps", [128, 512], F32)
            tq = [k.ps(pstk, "tqs0", [128, 128], BF16),
                  k.ps(pstk, "tqs1", [128, 128], BF16)]
            m0, u0 = k.c["sum_mm"], k.c["sum_up"]
            t0_, te0 = k.c["tr_mm"], k.c["tr_ev"]

            @blk.tensor
            def _(t):
                t = TEng(t)
                for ch in range(NCH):
                    gr, j = ch // 4, ch % 4
                    if j == 0:
                        if gr >= 1:
                            k.w(t, "sum_up", u0 + ch + 1)
                        self._khtg_mms(t, k, gr, tq, gr * 8, te0, t0_)
                    k.w(t, "sum_up", u0 + ch + 1)
                    k.w(t, "tr_ev", te0 + (gr + 1) * 8)
                    last = self._contrib_mms(t, ch, CP)
                    k.inc(last, "sum_mm", 1)

            @blk.vector
            def _(v):
                v = Eng(v)
                i = v.memset(k.S, 0.0)
                k.inc(i, "sum_up", 1)
                for ch in range(NCH):
                    gr, j = ch // 4, ch % 4
                    if j == 0:
                        self._khtg_evs(v, k, gr * 8, tq, t0_)
                    k.w(v, "sum_mm", m0 + ch + 1)
                    i = self._s_update(v, ch, CP)
                    k.inc(i, "sum_up", 1)
                v.tensor_copy(k.SB16B[1], k.S)

            @blk.gpsimd
            def _(g):
                for gi in prefetch:
                    self._q_load(g, gi)

    def ph_p(self):
        k = self
        T, NG = self.T, self.NG
        with self.nc.Block() as blk, ExitStack() as pstk:
            PP = [k.ps(pstk, "pp0", [128, 1024], F32),
                  k.ps(pstk, "pp1", [128, 1024], F32)]
            p0, e0 = k.c["p_mm"], k.c["pm_ev"]

            @blk.tensor
            def _(t):
                t = TEng(t)
                for gr in range(NG):
                    if gr >= 2:
                        k.w(t, "pm_ev", e0 + 8 * (gr - 1))
                    for sub in range(4):
                        for j in range(4):
                            ch = gr * 4 + j
                            for hs in range(8):
                                lhsT = k.KH[32 * sub:32 * sub + 32,
                                            hs * T + ch * CH:
                                            hs * T + (ch + 1) * CH]
                                rhs = k.QT[32 * sub:32 * sub + 32,
                                           hs * T + ch * CH:
                                           hs * T + (ch + 1) * CH]
                                last = t.matmul(
                                    PP[gr % 2][32 * j:32 * j + 32,
                                               hs * 128 + 32 * sub:
                                               hs * 128 + 32 * sub + 32],
                                    lhsT, rhs, start=True, stop=True,
                                    tile_position=(32 * sub, 32 * j))
                    k.inc(last, "p_mm", 1)

            @blk.vector
            def _(v):
                v = Eng(v)
                for gr in range(NG):
                    k.w(v, "p_mm", p0 + gr + 1)
                    for hs in range(8):
                        ev = v.tensor_tensor(
                            k.PM[:, gr * 1024 + hs * 128:
                                 gr * 1024 + (hs + 1) * 128],
                            PP[gr % 2][:, hs * 128:(hs + 1) * 128],
                            k.MASKP, ALU.mult)
                        k.inc(ev, "pm_ev", 1)

    def ph_chain(self):
        k = self
        T, NG, NCH = self.T, self.NG, self.NCH
        with self.nc.Block() as blk, ExitStack() as pstk:
            OP = k.ps(pstk, "opc", [128, 2048], F32)
            CPb = [k.ps(pstk, "cpc0", [128, 512], F32),
                   k.ps(pstk, "cpc1", [128, 512], F32)]
            tq = [k.ps(pstk, "tqc0", [128, 128], BF16),
                  k.ps(pstk, "tqc1", [128, 128], BF16)]
            m0, u0, e0 = k.c["c_mm"], k.c["c_up"], k.c["c_ev"]
            ci0 = k.c["c_oi"]
            s0g, v0g = k.c["h_s"], k.c["gn_v"]
            t0_, te0 = k.c["tr_mm"], k.c["tr_ev"]

            @blk.tensor
            def _(t):
                t = TEng(t)

                def contrib_part(x):
                    g = x // 4
                    if x % 4 == 0:
                        self._khtg_mms(t, k, g, tq, g * 8, te0, t0_)
                        k.w(t, "tr_ev", te0 + (g + 1) * 8)
                    if x >= 2:
                        k.w(t, "c_up", u0 + x - 1)
                    last = self._contrib_mms(t, x, CPb[x % 2], buf=g % 2)
                    k.inc(last, "c_mm", 1)

                contrib_part(0)
                for ch in range(NCH):
                    gr, j = ch // 4, ch % 4
                    if ch + 1 < NCH:
                        contrib_part(ch + 1)
                    if j == 0 and gr >= 1:
                        k.w(t, "c_ev", e0 + gr)
                    k.w(t, "c_up", u0 + ch)
                    last_i = None
                    for sub in range(4):
                        for hs in range(8):
                            h = 4 * hs + sub
                            lhsT = k.QT[32 * sub:32 * sub + 32,
                                        hs * T + ch * CH:hs * T + (ch + 1) * CH]
                            rhs = k.SB16B[(ch + 1) % 2][
                                32 * sub:32 * sub + 32, 64 * hs:64 * hs + 64]
                            last_i = t.matmul(OP[32 * j:32 * j + 32,
                                              64 * h:64 * h + 64],
                                              lhsT, rhs, start=(h % 8 == 0),
                                              stop=False,
                                              tile_position=(32 * sub, 32 * j))
                    k.inc(last_i, "c_oi", 1)
                    for h in range(NH):
                        hs, sub = h // 4, h % 4
                        lhsT = k.PM[32 * j:32 * j + 32,
                                    gr * 1024 + hs * 128 + sub * 32:
                                    gr * 1024 + hs * 128 + sub * 32 + 32]
                        rhs = k.V[32 * j:32 * j + 32,
                                  gr * DV + 64 * h:gr * DV + 64 * h + 64]
                        last_i = t.matmul(OP[32 * j:32 * j + 32,
                                           64 * h:64 * h + 64],
                                          lhsT, rhs, start=False,
                                          stop=(h % 8 == 7),
                                          tile_position=(32 * j, 32 * j))
                    k.inc(last_i, "c_oi", 1)

            @blk.scalar
            def _(sc):
                sc = Eng(sc)
                NGm = NCH // 4
                for gr in range(NGm):
                    if gr >= 1:
                        k.w(sc, "gn_v", v0g + 3 * (gr - 1) + 2)
                    k.w(sc, "c_ev", e0 + gr + 1)
                    a = sc.activation(k.SQC, k.Og[gr], AF.Square)
                    k.inc(a, "h_s", 1)
                    k.w(sc, "gn_v", v0g + 3 * gr + 2)
                    a = sc.activation(k.GNS[:, 64:96], k.GNS[:, 96:128],
                                      AF.Sqrt)
                    k.inc(a, "h_s", 1)

            @blk.vector
            def _(v):
                v = Eng(v)
                for ch in range(NCH):
                    gr, j = ch // 4, ch % 4
                    if ch == 0:
                        self._khtg_evs(v, k, 0, tq, t0_, buf=0)
                    if j == 3 and ch + 1 < NCH:
                        g2 = (ch + 1) // 4
                        self._khtg_evs(v, k, g2 * 8, tq, t0_, buf=g2 % 2)
                    k.w(v, "c_mm", m0 + ch + 1)
                    if ch >= 1:
                        k.w(v, "c_oi", ci0 + 2 * (ch - 1) + 1)
                    self._s_update(v, ch, CPb[ch % 2])
                    i = v.tensor_copy(k.SB16B[ch % 2], k.S)
                    k.inc(i, "c_up", 1)
                    if j == 3:
                        k.w(v, "c_oi", ci0 + 2 * ch + 2)
                        osl = k.Og[gr]
                        v.tensor_copy(osl, OP)
                        for h in range(NH):
                            i = v.scalar_tensor_tensor(
                                osl[:, 64 * h:64 * h + 64],
                                k.V[:, gr * DV + 64 * h:gr * DV + 64 * h + 64],
                                k.DTM[:, gr * 32 + h:gr * 32 + h + 1],
                                osl[:, 64 * h:64 * h + 64], ALU.mult, ALU.add)
                        k.inc(i, "c_ev", 1)
                        o3 = osl.rearrange("p (h v) -> p h v", h=NH)
                        v.tensor_reduce(k.GNS[:, 0:32].unsqueeze(2), o3,
                                        AX.X, ALU.add)
                        i = v.tensor_scalar(k.GNS[:, 0:32], k.GNS[:, 0:32],
                                            1.0 / 64, None, ALU.mult)
                        k.inc(i, "gn_v", 1)
                        k.w(v, "h_s", s0g + 2 * gr + 1)
                        v.tensor_reduce(k.GNS[:, 32:64].unsqueeze(2),
                                        k.SQC.rearrange("p (h v) -> p h v",
                                                        h=NH),
                                        AX.X, ALU.add)
                        v.tensor_scalar(k.GNS[:, 32:64], k.GNS[:, 32:64],
                                        1.0 / 64, EPS, ALU.mult, ALU.add)
                        v.tensor_tensor(k.GNS[:, 96:128], k.GNS[:, 0:32],
                                        k.GNS[:, 0:32], ALU.mult)
                        v.tensor_tensor(k.GNS[:, 96:128], k.GNS[:, 32:64],
                                        k.GNS[:, 96:128], ALU.subtract)
                        i = v.reciprocal(k.GNS[:, 96:128], k.GNS[:, 96:128])
                        k.inc(i, "gn_v", 1)
                        k.w(v, "h_s", s0g + 2 * gr + 2)
                        for h in range(NH):
                            v.tensor_scalar(osl[:, 64 * h:64 * h + 64],
                                            osl[:, 64 * h:64 * h + 64],
                                            k.GNS[:, h:h + 1],
                                            k.GNS[:, 64 + h:64 + h + 1],
                                            ALU.subtract, ALU.mult)
                        v.tensor_tensor(osl, osl, k.GNW, ALU.mult)
                        i = v.tensor_tensor(osl, osl, k.GNB, ALU.add)
                        k.inc(i, "gn_v", 1)

    def ph_gnconst(self):
        k = self
        with self.nc.Block() as blk:
            @blk.sync
            def _(s):
                n1 = k.inc(s.dma_start(out=k.GNW,
                                       in_=k.D["gnw_row"].broadcast_to((128, DV))),
                           "cst", 16)
                n2 = k.inc(s.dma_start(out=k.GNB,
                                       in_=k.D["gnb_row"].broadcast_to((128, DV))),
                           "cst", 16)
                k.w(s, "cst", n2)

    def ph_gn(self):
        k = self
        T, NT, NG = self.T, self.NT, self.NG
        NGH = max(1, NG // 2)
        with self.nc.Block() as blk:
            s0, v0 = k.c["h_s"], k.c["gn_v"]

            @blk.scalar
            def _(sc):
                sc = Eng(sc)
                for tt in range(NT):
                    osl = k.Og[tt]
                    if tt >= 1:
                        k.w(sc, "gn_v", v0 + 3 * (tt - 1) + 2)
                    a = sc.activation(k.SQB, osl, AF.Square)
                    k.inc(a, "h_s", 1)
                    k.w(sc, "gn_v", v0 + 3 * tt + 2)
                    a = sc.activation(k.GNS[:, 64:96], k.GNS[:, 96:128], AF.Sqrt)
                    k.inc(a, "h_s", 1)

            @blk.vector
            def _(v):
                v = Eng(v)
                for tt in range(NT):
                    osl = k.Og[tt]
                    o3 = osl.rearrange("p (h v) -> p h v", h=NH)
                    v.tensor_reduce(k.GNS[:, 0:32].unsqueeze(2), o3, AX.X, ALU.add)
                    i = v.tensor_scalar(k.GNS[:, 0:32], k.GNS[:, 0:32],
                                        1.0 / 64, None, ALU.mult)
                    k.inc(i, "gn_v", 1)
                    k.w(v, "h_s", s0 + 2 * tt + 1)
                    v.tensor_reduce(k.GNS[:, 32:64].unsqueeze(2),
                                    k.SQB.rearrange("p (h v) -> p h v", h=NH),
                                    AX.X, ALU.add)
                    v.tensor_scalar(k.GNS[:, 32:64], k.GNS[:, 32:64],
                                    1.0 / 64, EPS, ALU.mult, ALU.add)
                    v.tensor_tensor(k.GNS[:, 96:128], k.GNS[:, 0:32],
                                    k.GNS[:, 0:32], ALU.mult)
                    v.tensor_tensor(k.GNS[:, 96:128], k.GNS[:, 32:64],
                                    k.GNS[:, 96:128], ALU.subtract)
                    i = v.reciprocal(k.GNS[:, 96:128], k.GNS[:, 96:128])
                    k.inc(i, "gn_v", 1)
                    k.w(v, "h_s", s0 + 2 * tt + 2)
                    on = osl
                    for h in range(NH):
                        v.tensor_scalar(on[:, 64 * h:64 * h + 64],
                                        osl[:, 64 * h:64 * h + 64],
                                        k.GNS[:, h:h + 1],
                                        k.GNS[:, 64 + h:64 + h + 1],
                                        ALU.subtract, ALU.mult)
                    v.tensor_tensor(on, on, k.GNW, ALU.mult)
                    i = v.tensor_tensor(on, on, k.GNB, ALU.add)
                    k.inc(i, "gn_v", 1)

    def ph_trans(self, src, dst, nvc):
        k = self
        T, NT = self.T, self.NT
        with self.nc.Block() as blk, ExitStack() as pstk:
            tps = [k.ps(pstk, "tpt0", [128, 128], BF16),
                   k.ps(pstk, "tpt1", [128, 128], BF16)]
            t0_, e0 = k.c["tr_mm"], k.c["tr_ev"]

            @blk.tensor
            def _(t):
                t = TEng(t)
                n = 0
                for tt in range(NT):
                    for vc in range(nvc):
                        if n >= 2:
                            k.w(t, "tr_ev", e0 + n - 1)
                        mm = t.transpose(
                            tps[n % 2],
                            src[:, tt * (nvc * 128) + vc * 128:
                                tt * (nvc * 128) + (vc + 1) * 128], k.IDENT)
                        k.inc(mm, "tr_mm", 1)
                        n += 1

            @blk.vector
            def _(v):
                v = Eng(v)
                n = 0
                for tt in range(NT):
                    for vc in range(nvc):
                        k.w(v, "tr_mm", t0_ + n + 1)
                        ev = v.tensor_copy(
                            dst[:, vc * T + tt * 128:vc * T + (tt + 1) * 128],
                            tps[n % 2])
                        k.inc(ev, "tr_ev", 1)
                        n += 1

    def ph_lateg_xin(self):
        k = self
        T = self.T
        with self.nc.Block() as blk, ExitStack() as pstk:
            mup = [k.ps(pstk, "mupg0", [128, T], F32),
                   k.ps(pstk, "mupg1", [128, T], F32)]
            mu0, xin0 = k.c["mu_mm"], k.c["xin_v"]

            @blk.tensor
            def _(t):
                t = TEng(t)
                for cc in range(NCC):
                    if cc >= 2:
                        k.w(t, "xin_v", xin0 + cc - 1)
                    mm = k.mm(t, mup[cc % 2],
                              k.WX2B[0:32, cc * 128:(cc + 1) * 128],
                              k.Z1[0:32, 0:T], start=True, stop=True, tp=(0, 0))
                    k.inc(mm, "mu_mm", 1)

            @blk.vector
            def _(v):
                v = Eng(v)
                for cc in range(NCC):
                    k.w(v, "mu_mm", mu0 + cc + 1)
                    i2 = self._xin_dve(v, cc, 0, T, mup[cc % 2], k.XIN, cc * T)
                    k.inc(i2, "xin_v", 1)

    def ph_lateg_mm(self):
        k = self
        T = self.T
        with self.nc.Block() as blk, ExitStack() as pstk:
            gp = [k.ps(pstk, "gp0", [128, T], F32),
                  k.ps(pstk, "gp1", [128, T], F32)]
            m0, s0, v0 = k.c["lg_mm"], k.c["lg_s"], k.c["lg_v"]
            gq = [0, 0]
            gqv = []

            @blk.gpsimd
            def _(g):
                for gs in range(16):
                    if gs >= 2:
                        k.w(g, "lg_mm", m0 + gs - 1)
                    v_ = k.inc(g.dma_start(out=k.WGSL[gs % 2],
                                           in_=k.D["wgs"][gs]), f"gq{gs % 2}", 16)
                    gqv.append(v_)
                for dc in range(10):
                    k.inc(g.dma_start(out=k._wo_q(dc), in_=k.D["wos"][dc]),
                          "wor", 16)

            @blk.tensor
            def _(t):
                t = TEng(t)
                for gs in range(16):
                    k.w(t, f"gq{gs % 2}", gqv[gs])
                    if gs >= 2:
                        k.w(t, "lg_s", s0 + gs - 1)
                    for cc in range(NCC):
                        last = k.mm(t, gp[gs % 2],
                                    k.WGSL[gs % 2][:, cc * 128:(cc + 1) * 128],
                                    k.XIN[:, cc * T:(cc + 1) * T],
                                    start=(cc == 0), stop=(cc == NCC - 1))
                    k.inc(last, "lg_mm", 1)

            @blk.scalar
            def _(sc):
                sc = Eng(sc)
                for gs in range(16):
                    k.w(sc, "lg_mm", m0 + gs + 1)
                    if gs >= 1:
                        k.w(sc, "lg_v", v0 + gs)
                    a = sc.activation(k.SILS[:, 0:T], gp[gs % 2], AF.Silu)
                    k.inc(a, "lg_s", 1)

            @blk.vector
            def _(v):
                v = Eng(v)
                for gs in range(16):
                    k.w(v, "lg_s", s0 + gs + 1)
                    sl = k.ONFM[:, gs * T:(gs + 1) * T]
                    i = v.tensor_tensor(sl, sl, k.SILS[:, 0:T], ALU.mult)
                    k.inc(i, "lg_v", 1)

    def ph_wo(self):
        k = self
        T, NT = self.T, self.NT
        Vf = self.V.bitcast(F32)
        WOC = Vf[:, 4096:4096 + HID]
        with self.nc.Block() as blk, ExitStack() as pstk:
            WOP = k.ps(pstk, "wop", [128, 2048], F32)
            tpx = [k.ps(pstk, "tpx0", [128, 128], BF16),
                   k.ps(pstk, "tpx1", [128, 128], BF16)]
            xldv = []
            X1T = self.X1T
            wo0, wv0 = k.c["wo_mm"], k.c["wo_v"]
            s0 = k.c["s_a"]
            t0_, e0 = k.c["tr_mm"], k.c["tr_ev"]
            x10 = k.c["x1s"]
            wr0 = k.c["wor"]
            q00 = k.c["woq0"]

            @blk.gpsimd
            def _(g):
                for dc in range(10, 16):
                    k.inc(g.dma_start(out=k._wo_q(dc), in_=k.D["wos"][dc]),
                          "woq0", 16)

            @blk.sync
            def _(s):
                for tt in range(NT):
                    if tt >= 2:
                        k.w(s, "wo_v", wv0 + 5 * (tt - 1))
                    v_ = k.inc(s.dma_start(
                        out=k.XT[tt % 2],
                        in_=k.D["xsl"][1 + LB + tt * 128:1 + LB + (tt + 1) * 128, :]),
                        f"xld{tt % 2}", 16)
                    xldv.append(v_)
                    k.w(s, "wo_v", wv0 + 5 * tt + 3)
                    k.inc(s.dma_start(out=k.D["x1d"][tt * 128:(tt + 1) * 128, :],
                                      in_=X1T), "x1s", 16)
                k.w(s, "x1s", k.c["x1s"])

            @blk.tensor
            def _(t):
                t = TEng(t)
                for tt in range(NT):
                    if tt >= 1:
                        k.w(t, "wo_v", wv0 + 5 * (tt - 1) + 1)
                    for dc in range(NCC):
                        if tt == 0 and dc == 0:
                            k.w(t, "wor", wr0)
                        if tt == 0 and dc == 10:
                            k.w(t, "woq0", q00 + 96)
                        lhsT = k.ONFM[:, dc * T + tt * 128:dc * T + (tt + 1) * 128]
                        for q in range(4):
                            last = t.matmul(WOP[:, q * 512:(q + 1) * 512],
                                            lhsT,
                                            k._wo_q(dc)[:, q * 512:(q + 1) * 512],
                                            start=(dc == 0), stop=(dc == NCC - 1))
                        k.inc(last, "wo_mm", 1)
                    if tt >= 1:
                        k.w(t, "wo_v", wv0 + 5 * (tt - 1) + 5)
                        for cc in range(NCC):
                            n = (tt - 1) * NCC + cc
                            if n >= 2:
                                k.w(t, "tr_ev", e0 + n - 1)
                            mm = t.transpose(tpx[n % 2],
                                             k.SCR[0][:, cc * 128:(cc + 1) * 128],
                                             k.IDENT)
                            k.inc(mm, "tr_mm", 1)
                k.w(t, "wo_v", wv0 + 5 * (NT - 1) + 5)
                for cc in range(NCC):
                    n = (NT - 1) * NCC + cc
                    if n >= 2:
                        k.w(t, "tr_ev", e0 + n - 1)
                    mm = t.transpose(tpx[n % 2],
                                     k.SCR[0][:, cc * 128:(cc + 1) * 128],
                                     k.IDENT)
                    k.inc(mm, "tr_mm", 1)

            @blk.scalar
            def _(sc):
                sc = Eng(sc)
                for tt in range(NT):
                    k.w(sc, "wo_v", wv0 + 5 * tt + 1)
                    a = sc.activation(k.SCF, WOC, AF.Square,
                                      accum_out=k.ST1[:, 4:5])
                    k.inc(a, "s_a", 1)
                    k.w(sc, "wo_v", wv0 + 5 * tt + 2)
                    a = sc.activation(k.ST1[:, 5:6], k.ST1[:, 6:7], AF.Sqrt)
                    k.inc(a, "s_a", 1)
                    k.w(sc, "wo_v", wv0 + 5 * tt + 3)
                    a = sc.activation(k.SCF, X1T, AF.Square,
                                      accum_out=k.ST1[:, 12:13])
                    k.inc(a, "s_a", 1)
                    k.w(sc, "wo_v", wv0 + 5 * tt + 4)
                    a = sc.activation(k.ST1[:, 13:14], k.ST1[:, 14:15], AF.Sqrt)
                    k.inc(a, "s_a", 1)

            @blk.vector
            def _(v):
                v = Eng(v)
                for tt in range(NT):
                    k.w(v, "wo_mm", wo0 + (tt + 1) * NCC)
                    i = v.tensor_copy(WOC, WOP)
                    k.inc(i, "wo_v", 1)
                    v.tensor_reduce(k.ST1[:, 0:1], WOC, AX.X, ALU.add)
                    v.tensor_scalar(k.ST1[:, 1:2], k.ST1[:, 0:1], 1.0 / HID,
                                    None, ALU.mult)
                    k.w(v, "s_a", s0 + 4 * tt + 1)
                    v.tensor_scalar(k.ST1[:, 2:3], k.ST1[:, 4:5], 1.0 / HID,
                                    EPS, ALU.mult, ALU.add)
                    v.tensor_tensor(k.ST1[:, 3:4], k.ST1[:, 1:2], k.ST1[:, 1:2],
                                    ALU.mult)
                    v.tensor_tensor(k.ST1[:, 2:3], k.ST1[:, 2:3], k.ST1[:, 3:4],
                                    ALU.subtract)
                    i = v.reciprocal(k.ST1[:, 6:7], k.ST1[:, 2:3])
                    k.inc(i, "wo_v", 1)
                    k.w(v, "s_a", s0 + 4 * tt + 2)
                    k.w(v, f"xld{tt % 2}", xldv[tt])
                    v.tensor_scalar(k.SCF, WOC, k.ST1[:, 1:2], k.ST1[:, 5:6],
                                    ALU.subtract, ALU.mult)
                    if tt >= 1:
                        k.w(v, "x1s", x10 + 16 * tt)
                    i = v.tensor_tensor(X1T, k.SCF, k.XT[tt % 2], ALU.add)
                    k.inc(i, "wo_v", 1)
                    v.tensor_reduce(k.ST1[:, 8:9], X1T, AX.X, ALU.add)
                    v.tensor_scalar(k.ST1[:, 9:10], k.ST1[:, 8:9], 1.0 / HID,
                                    None, ALU.mult)
                    k.w(v, "s_a", s0 + 4 * tt + 3)
                    v.tensor_scalar(k.ST1[:, 10:11], k.ST1[:, 12:13], 1.0 / HID,
                                    EPS, ALU.mult, ALU.add)
                    v.tensor_tensor(k.ST1[:, 11:12], k.ST1[:, 9:10],
                                    k.ST1[:, 9:10], ALU.mult)
                    v.tensor_tensor(k.ST1[:, 10:11], k.ST1[:, 10:11],
                                    k.ST1[:, 11:12], ALU.subtract)
                    i = v.reciprocal(k.ST1[:, 14:15], k.ST1[:, 10:11])
                    k.inc(i, "wo_v", 1)
                    k.w(v, "s_a", s0 + 4 * tt + 4)
                    if tt >= 1:
                        for cc in range(NCC):
                            n = (tt - 1) * NCC + cc
                            k.w(v, "tr_mm", t0_ + n + 1)
                            ev = v.tensor_copy(
                                k.XFF[:, cc * T + (tt - 1) * 128:
                                      cc * T + tt * 128],
                                tpx[n % 2])
                            k.inc(ev, "tr_ev", 1)
                        k.w(v, "tr_mm", t0_ + tt * NCC)
                    i = v.tensor_scalar(k.SCR[0], X1T, k.ST1[:, 9:10],
                                        k.ST1[:, 13:14], ALU.subtract, ALU.mult)
                    k.inc(i, "wo_v", 1)
                for cc in range(NCC):
                    n = (NT - 1) * NCC + cc
                    k.w(v, "tr_mm", t0_ + n + 1)
                    ev = v.tensor_copy(
                        k.XFF[:, cc * T + (NT - 1) * 128:cc * T + NT * 128],
                        tpx[n % 2])
                    k.inc(ev, "tr_ev", 1)

    def ph_router(self):
        k = self
        T, NT = self.T, self.NT
        BIG = 65536.0
        with self.nc.Block() as blk, ExitStack() as pstk:
            RTP = [k.ps(pstk, "rtp0", [128, 16], F32),
                   k.ps(pstk, "rtp1", [128, 16], F32)]
            SGP = [k.ps(pstk, "sgp0", [128, 8], F32),
                   k.ps(pstk, "sgp1", [128, 8], F32)]
            m0, v0, s0 = k.c["rt_mm"], k.c["rt_v"], k.c["h_s"]

            @blk.tensor
            def _(t):
                t = TEng(t)
                for tt in range(NT):
                    if tt >= 2:
                        k.w(t, "rt_v", v0 + 2 * (tt - 1))
                        k.w(t, "h_s", s0 + 2 * (tt - 1))
                    for cc in range(NCC):
                        lhsT = k.XFF[:, cc * T + tt * 128:cc * T + (tt + 1) * 128]
                        t.matmul(RTP[tt % 2][:, 0:NE], lhsT,
                                 k.GWSB[:, cc * NE:(cc + 1) * NE],
                                 start=(cc == 0), stop=(cc == NCC - 1))
                        last = t.matmul(SGP[tt % 2][:, 0:1], lhsT,
                                        k.SGWB[:, cc:cc + 1],
                                        start=(cc == 0), stop=(cc == NCC - 1))
                    k.inc(last, "rt_mm", 1)

            @blk.scalar
            def _(sc):
                sc = Eng(sc)
                for tt in range(NT):
                    k.w(sc, "rt_v", v0 + 2 * tt + 1)
                    a = sc.activation(k.PR8[:, 0:8], RTP[tt % 2][:, 0:NE],
                                      AF.Exp, bias=k.PR8[:, 8:9])
                    k.inc(a, "h_s", 1)
                    a = sc.activation(k.SGC[:, tt:tt + 1], SGP[tt % 2][:, 0:1],
                                      AF.Sigmoid)
                    k.inc(a, "h_s", 1)

            @blk.vector
            def _(v):
                v = Eng(v)
                for tt in range(NT):
                    k.w(v, "rt_mm", m0 + tt + 1)
                    v.reduce_max(k.PR8[:, 8:9], RTP[tt % 2][:, 0:NE], AX.X)
                    i = v.tensor_scalar(k.PR8[:, 8:9], k.PR8[:, 8:9], -1.0,
                                        None, ALU.mult)
                    k.inc(i, "rt_v", 1)
                    k.w(v, "h_s", s0 + 2 * tt + 1)
                    v.tensor_reduce(k.PR8[:, 9:10], k.PR8[:, 0:8], AX.X, ALU.add)
                    v.reciprocal(k.PR8[:, 9:10], k.PR8[:, 9:10])
                    v.tensor_scalar(k.PR8[:, 0:8], k.PR8[:, 0:8],
                                    k.PR8[:, 9:10], None, ALU.mult)
                    v.reduce_max(k.PR8[:, 10:11], k.PR8[:, 0:8], AX.X)
                    v.tensor_scalar(k.STW, k.PR8[:, 0:8], k.PR8[:, 10:11],
                                    None, ALU.is_ge)
                    v.scalar_tensor_tensor(k.STW, k.STW, -BIG, k.PR8[:, 0:8],
                                           ALU.mult, ALU.add)
                    v.reduce_max(k.PR8[:, 11:12], k.STW, AX.X)
                    v.tensor_scalar(k.STW, k.PR8[:, 0:8], k.PR8[:, 11:12],
                                    None, ALU.is_ge)
                    i = v.tensor_tensor(k.CMB[:, tt * 8:(tt + 1) * 8],
                                        k.PR8[:, 0:8], k.STW, ALU.mult)
                    k.inc(i, "rt_v", 1)

    def _wpj_regions(self):
        T = self.T
        regs = []
        regs.append((self.XNT, 0, (NCC * (T + 1)) // HID))
        regs.append((self.arB, 0, 4))
        vstart = ((NFC * T + HID - 1) // HID) * HID
        regs.append((self.V, vstart, (self.AW - vstart) // HID))
        out = []
        left = NFC
        for ten, o0, n in regs:
            take = min(left, n)
            if take > 0:
                out.append((ten, o0, take))
            left -= take
        assert left == 0, "wpj capacity"
        return out

    def _wpj_spans(self):
        out = []
        fc0 = 0
        for ten, o0, n in self._wpj_regions():
            out.append((ten[:, o0:o0 + n * HID], fc0, n))
            fc0 += n
        return out

    def _wpj(self, fc):
        i = fc
        for ten, o0, n in self._wpj_regions():
            if i < n:
                return ten[:, o0 + i * HID:o0 + (i + 1) * HID]
            i -= n
        raise IndexError(fc)

    def ph_experts(self):
        k = self
        T, NT = self.T, self.NT
        NEX = NE + 4
        with self.nc.Block() as blk, ExitStack() as pstk:
            HPA = k.ps(pstk, "hpa", [128, 2048], F32)
            PA0 = k.ps(pstk, "pa", [128, 2048], F32)
            HP1 = [HPA[:, 0:T], PA0[:, 0:T]]
            HP2 = [HPA[:, 1024:1024 + T], PA0[:, 1024:1024 + T]]
            PA = [PA0, HPA]
            hm0, hs0, hv0 = k.c["h_mm"], k.c["h_s"], k.c["h_v"]
            ym0, yv0 = k.c["y_mm"], k.c["y_v"]
            fqv, f2qv, pqv = [], [], []

            def wsrc(e):
                if e < NE:
                    return (lambda fc: k.D["fc1s"][e, fc],
                            lambda fc: k.D["fc2s"][e, fc],
                            lambda fc0, nfc: k.D["pjs"][e, fc0:fc0 + nfc])
                st = e - NE
                return (lambda fc: k.D["sf1s"][st * NFC + fc],
                        lambda fc: k.D["sf2s"][st * NFC + fc],
                        lambda fc0, nfc: k.D["spjs"][st * NFC + fc0:
                                                     st * NFC + fc0 + nfc])

            @blk.gpsimd
            def _(g):
                for e in range(NEX):
                    f1, f2, pjd = wsrc(e)
                    for fc in range(NFC):
                        n = e * NFC + fc
                        if n >= 4:
                            k.w(g, "h_mm", hm0 + n - 3)
                        v1 = k.inc(g.dma_start(out=k.FC1SL[n % 4], in_=f1(fc)),
                                   f"fq{n % 4}", 16)
                        v2 = k.inc(g.dma_start(out=k.FC2SL[n % 4], in_=f2(fc)),
                                   f"f2q{n % 4}", 16)
                        fqv.append(v1)
                        f2qv.append(v2)
                        if fc == (0 if e == 0 else 4):
                            if e >= 1:
                                k.w(g, "y_mm", ym0 + e * NT)
                            vs = [None] * 4
                            for fc_ in range(NFC):
                                si = fc_ % 4
                                vs[si] = k.inc(
                                    g.dma_start(
                                        out=self._wpj(fc_),
                                        in_=pjd(fc_, 1).rearrange(
                                            "f p c -> p (f c)")),
                                    f"pq{si}", 16)
                            pqv.append(vs)

            @blk.tensor
            def _(t):
                t = TEng(t)
                for e in range(NEX):
                    if e >= 1:
                        k.w(t, "y_v", yv0 + e * NT)
                    for fc in range(NFC):
                        n = e * NFC + fc
                        k.w(t, f"fq{n % 4}", fqv[n])
                        k.w(t, f"f2q{n % 4}", f2qv[n])
                        if n >= 2:
                            k.w(t, "h_v", hv0 + n - 1)
                        for cc in range(NCC):
                            k.mm(t, HP1[n % 2],
                                 k.FC1SL[n % 4][:, cc * 128:(cc + 1) * 128],
                                 k.XFF[:, cc * T:(cc + 1) * T],
                                 start=(cc == 0), stop=(cc == NCC - 1))
                        last = None
                        for cc in range(NCC):
                            last = k.mm(t, HP2[n % 2],
                                        k.FC2SL[n % 4][:, cc * 128:(cc + 1) * 128],
                                        k.XFF[:, cc * T:(cc + 1) * T],
                                        start=(cc == 0), stop=(cc == NCC - 1))
                        k.inc(last, "h_mm", 1)
                    for tt in range(NT):
                        n2 = e * NT + tt
                        k.w(t, "h_v", hv0 + (e + 1) * NFC)
                        if n2 >= 2:
                            k.w(t, "y_v", yv0 + n2 - 1)
                        if tt == 0:
                            for si, v_ in enumerate(pqv[e]):
                                k.w(t, f"pq{si}", v_)
                        for fc in range(NFC):
                            lhsT = k.Ht[:, fc * T + tt * 128:
                                        fc * T + (tt + 1) * 128]
                            for q in range(4):
                                last = t.matmul(
                                    PA[n2 % 2][:, q * 512:(q + 1) * 512],
                                    lhsT, self._wpj(fc)[:, q * 512:(q + 1) * 512],
                                    start=(fc == 0), stop=(fc == NFC - 1))
                        k.inc(last, "y_mm", 1)

            @blk.scalar
            def _(sc):
                sc = Eng(sc)
                for e in range(NEX):
                    for fc in range(NFC):
                        n = e * NFC + fc
                        k.w(sc, "h_mm", hm0 + n + 1)
                        if n >= 1:
                            k.w(sc, "h_v", hv0 + n)
                        a = sc.activation(k.SILS[:, 0:T], HP1[n % 2], AF.Silu)
                        k.inc(a, "h_s", 1)

            @blk.vector
            def _(v):
                v = Eng(v)
                for e in range(NEX):
                    for fc in range(NFC):
                        n = e * NFC + fc
                        k.w(v, "h_s", hs0 + n + 1)
                        k.w(v, "h_mm", hm0 + n + 1)
                        i = v.tensor_tensor(k.Ht[:, fc * T:(fc + 1) * T],
                                            k.SILS[:, 0:T], HP2[n % 2],
                                            ALU.mult)
                        k.inc(i, "h_v", 1)
                    for tt in range(NT):
                        n2 = e * NT + tt
                        k.w(v, "y_mm", ym0 + n2 + 1)
                        scal = (k.CMB[:, tt * 8 + e:tt * 8 + e + 1] if e < NE
                                else k.SGC[:, tt:tt + 1])
                        pa_ = PA[n2 % 2]
                        if e == 0:
                            i = v.tensor_scalar(k.Yt[tt], pa_, scal, None,
                                                ALU.mult)
                        else:
                            i = v.scalar_tensor_tensor(k.Yt[tt], pa_, scal,
                                                       k.Yt[tt], ALU.mult,
                                                       ALU.add)
                        k.inc(i, "y_v", 1)

    def ph_out(self):
        k = self
        T, NT = self.T, self.NT
        with self.nc.Block() as blk:
            x1v = []
            odv = []
            s0 = k.c["s_a"]
            v0 = k.c["wo_v"]
            X1L = [self.arCf[:, 0:HID], self.arCf[:, HID:2 * HID]]
            OTS = [self.arCf[:, 2 * HID:3 * HID], self.arCf[:, 3 * HID:4 * HID]]
            TMP3 = self.arBf[:, 2 * HID:3 * HID]

            @blk.sync
            def _(s):
                for tt in range(NT):
                    if tt >= 2:
                        k.w(s, "wo_v", v0 + 2 * (tt - 1))
                    v_ = k.inc(s.dma_start(
                        out=X1L[tt % 2],
                        in_=k.D["x1d"][tt * 128:(tt + 1) * 128, :]),
                        f"xld{tt % 2}", 16)
                    x1v.append(v_)
                    k.w(s, "wo_v", v0 + 2 * tt + 2)
                    o_ = k.inc(s.dma_start(out=k.out[tt * 128:(tt + 1) * 128, :],
                                           in_=OTS[tt % 2]), "o_dma", 16)
                    odv.append(o_)

            @blk.scalar
            def _(sc):
                sc = Eng(sc)
                for tt in range(NT):
                    a = sc.activation(k.SQB, k.Yt[tt], AF.Square,
                                      accum_out=k.ST1[:, 4:5])
                    k.inc(a, "s_a", 1)
                    k.w(sc, "wo_v", v0 + 2 * tt + 1)
                    a = sc.activation(k.ST1[:, 5:6], k.ST1[:, 6:7], AF.Sqrt)
                    k.inc(a, "s_a", 1)

            @blk.vector
            def _(v):
                v = Eng(v)
                for tt in range(NT):
                    yt = k.Yt[tt]
                    v.tensor_reduce(k.ST1[:, 0:1], yt, AX.X, ALU.add)
                    v.tensor_scalar(k.ST1[:, 1:2], k.ST1[:, 0:1], 1.0 / HID,
                                    None, ALU.mult)
                    k.w(v, "s_a", s0 + 2 * tt + 1)
                    v.tensor_scalar(k.ST1[:, 2:3], k.ST1[:, 4:5], 1.0 / HID,
                                    EPS, ALU.mult, ALU.add)
                    v.tensor_tensor(k.ST1[:, 3:4], k.ST1[:, 1:2], k.ST1[:, 1:2],
                                    ALU.mult)
                    v.tensor_tensor(k.ST1[:, 2:3], k.ST1[:, 2:3], k.ST1[:, 3:4],
                                    ALU.subtract)
                    i = v.reciprocal(k.ST1[:, 6:7], k.ST1[:, 2:3])
                    k.inc(i, "wo_v", 1)
                    k.w(v, "s_a", s0 + 2 * tt + 2)
                    k.w(v, f"xld{tt % 2}", x1v[tt])
                    if tt >= 2:
                        k.w(v, "o_dma", odv[tt - 2])
                    v.tensor_scalar(TMP3, yt, k.ST1[:, 1:2], k.ST1[:, 5:6],
                                    ALU.subtract, ALU.mult)
                    i = v.tensor_tensor(OTS[tt % 2], TMP3, X1L[tt % 2], ALU.add)
                    k.inc(i, "wo_v", 1)

    def build(self, stop=None):
        self._stop = stop
        self._phn = 0
        self.qfree = {}
        nrk = DK // QW
        nv = DV // QW
        self._q_list = ([("wkq", i) for i in range(nrk)]
                        + [("wvq", i) for i in range(nv)]
                        + [("wrq", i) for i in range(nrk)]
                        + [("wkq", i) for i in range(nrk)]
                        + [("wvq", i) for i in range(nv)])
        Tm = self.T

        self._bind_views(LB)
        self._go(self.ph_load_consts, )
        self._go(self.ph_ln0, off=0)
        self._go(self.ph_z, )
        self._go(self.ph_branch_fm, 2, 0, self.Kb, DK, prefetch=[2, 3, 4, 5])
        self._w_prefetch = []
        self._go(self.ph_branch_fm, 1, None, self.THB, GLO, prefetch=[])
        self._go(self.ph_w_decay, extras=False)
        self._go(self.ph_branch_v, qb=4, prefetch=list(range(6, 12)))
        self._go(self.ph_sum, prefetch=[12, 13])
        self.dbg("sinit", self.S)

        self._bind_views(Tm)
        self._go(self.ph_ln0, off=LB)
        self.dbg("xnt", self.XNT)
        self._go(self.ph_z, )
        self.dbg("z0", self.Z0)
        self._go(self.ph_branch_fm, 0, 12, self.Rb, DK, prefetch=[14, 15, 16, 17])
        self._go(self.ph_branch_fm, 2, 16, self.Kb, DK, prefetch=[18, 19, 20, 21])
        self.dbg("rb", self.Rb)
        self.dbg("kb", self.Kb)
        self._w_prefetch = []
        self._go(self.ph_branch_fm, 1, None, self.THB, GLO, prefetch=[])
        self.dbg("thb", self.THB[0:64, 0:self.T])
        self._go(self.ph_w_decay, extras=True)
        self.dbg("qt", self.QT)
        self.dbg("kh", self.KH)
        self.dbg("dec", self.DEC)
        self.dbg("dtm", self.DTM)
        self._go(self.ph_branch_v, qb=20, prefetch=list(range(22, 28)))
        self.dbg("vtm", self.V[:, 0:self.NG * DV])
        self._go(self.ph_gnconst, )
        self._go(self.ph_p, )
        self.dbg("pm", self.PM)
        self._go(self.ph_chain, )
        self.dbg("ontm", self.ONTM)
        self._go(self.ph_trans, self.ONTM, self.ONFM, NCC)
        self._go(self.ph_lateg_xin, )
        self._go(self.ph_lateg_mm, )
        self.dbg("og", self.ONFM)
        self._go(self.ph_wo, )
        self.dbg("xff", self.XFF)
        self._go(self.ph_router, )
        self.dbg("cmb", self.CMB)
        self.dbg("sgc", self.SGC)
        self._go(self.ph_experts, )
        self.dbg("y0", self.Yt[0])
        self._go(self.ph_out, )
        self.finish()
        return self.nc

    def _go(self, fn, *a, **kw):
        if self._stop is not None and self._phn >= self._stop:
            return
        self._phn += 1
        fn(*a, **kw)

    def finish(self):
        k = self
        with self.nc.Block() as blk:
            @blk.sync
            def _(s):
                k.w(s, "o_dma", k.c["o_dma"])
        self.stack.close()


def _t(a):
    return np.ascontiguousarray(a)


def tiles_cm(w, out_w):
    return _t(w.reshape(NCC, 128, out_w).transpose(1, 0, 2).reshape(128, NCC * out_w))


def host_prepare(inputs, T, n_cores):
    I = {kk: np.asarray(vv) for kk, vv in inputs.items()}
    x = I["L0_attn_hidden_states"].reshape(-1, HID).astype(np.float32)
    W = {}
    W["wx1"] = tiles_cm(I["L0_attn_Wx1"].astype(np.float32), 160).astype(BF)
    wx2 = I["L0_attn_Wx2"].astype(np.float32)
    W["wx2a"] = _t(wx2[0:4].reshape(128, HID)).astype(BF)
    W["wx2b"] = _t(wx2[4]).astype(BF)
    W["ww1"] = tiles_cm(I["L0_attn_Ww1"].astype(np.float32), GLO).astype(BF)
    W["ww2"] = _t(I["L0_attn_Ww2"].astype(np.float32)).astype(BF)

    def quanta(w):
        out_w = w.shape[1]
        r = w.reshape(NCC, 128, out_w)
        return _t(np.stack(
            [r[:, :, i * QW:(i + 1) * QW].transpose(1, 0, 2).reshape(128, NCC * QW)
             for i in range(out_w // QW)]))

    W["wrq"] = quanta(I["L0_attn_Wr"].astype(np.float32))
    W["wkq"] = quanta(I["L0_attn_Wk"].astype(np.float32))
    W["wvq"] = quanta(I["L0_attn_Wv"].astype(np.float32))
    wg = I["L0_attn_Wg"].astype(np.float32).reshape(NCC, 128, 16, 128)
    W["wgs"] = _t(wg.transpose(2, 1, 0, 3).reshape(16, 128, NCC * 128))
    W["wos"] = _t(I["L0_attn_Wo"].astype(np.float32).reshape(16, 128, HID))
    W["bw"] = _t(I["L0_attn_bw"].astype(np.float32).reshape(8, 128).T)
    W["bonus"] = _t(I["L0_attn_bonus"].astype(np.float32).reshape(-1).reshape(8, 128).T)
    W["muxc"] = _t(I["L0_attn_mu_x"].astype(np.float32).reshape(NCC, 128).T)
    W["gnw_row"] = _t(np.tile(I["L0_attn_gn_w"].astype(np.float32), NH)[None, :])
    W["gnb_row"] = _t(np.tile(I["L0_attn_gn_b"].astype(np.float32), NH)[None, :])
    W["ident"] = np.eye(128, dtype=BF)
    p = np.arange(128)[:, None] % 32
    f = np.arange(128)[None, :] % 32
    W["maskp"] = _t((p < f).astype(BF))
    W["blkh4"] = _t((np.arange(128)[:, None] // 32 == np.arange(4)[None, :]).astype(BF))
    W["onesb"] = np.ones((128, 1), BF)
    W["gws"] = tiles_cm(I["L0_ffn_gate_w"].astype(np.float32).T, NE)
    W["sgws"] = tiles_cm(I["L0_ffn_shared_gate_w"].astype(np.float32).T, 1).reshape(128, NCC)

    def fc_tiles(w):
        F_ = w.shape[0]
        wt = w.T.reshape(NCC, 128, F_ // 128, 128)
        return _t(wt.transpose(2, 1, 0, 3).reshape(F_ // 128, 128, NCC * 128))

    fc1 = I["L0_ffn_expert_fc1"].astype(np.float32)
    fc2 = I["L0_ffn_expert_fc2"].astype(np.float32)
    pj = I["L0_ffn_expert_proj"].astype(np.float32)
    W["fc1s"] = _t(np.stack([fc_tiles(fc1[e]) for e in range(NE)]))
    W["fc2s"] = _t(np.stack([fc_tiles(fc2[e]) for e in range(NE)]))
    W["pjs"] = _t(np.stack([pj[e].T.reshape(NFC, 128, HID) for e in range(NE)]))
    W["sf1s"] = fc_tiles(I["L0_ffn_shared_fc1"].astype(np.float32))
    W["sf2s"] = fc_tiles(I["L0_ffn_shared_fc2"].astype(np.float32))
    W["spjs"] = _t(I["L0_ffn_shared_proj"].astype(np.float32).T.reshape(4 * NFC, 128, HID))

    maps = []
    for c in range(n_cores):
        m = dict(W)
        xs = np.zeros((1 + LB + T, HID), np.float32)
        t0 = c * T
        lo = max(0, t0 - LB - 1)
        span = x[lo:t0 + T]
        xs[1 + LB + T - span.shape[0]:] = span
        m["xsl"] = _t(xs)
        maps.append(m)
    return maps


def kernel(**inputs):
    T = 8192 // N_CORES
    maps = host_prepare(inputs, T, N_CORES)
    kb = KB(T, N_CORES, debug=False)
    nc = kb.build()
    res = run_bass_kernel_spmd(nc, maps, core_ids=list(range(N_CORES)))
    out = np.concatenate([r["out"] for r in res.results], axis=0)
    return out.reshape(1, 8192, HID)
